# version 20
# baseline (speedup 1.0000x reference)
import numpy as np
import concourse.bass as bass
import concourse.mybir as mybir
from concourse.bass_core_rust import IndirectOffsetOnAxis
from concourse.bass_utils import run_bass_kernel_spmd
from contextlib import ExitStack

F32 = mybir.dt.float32
BF16 = mybir.dt.bfloat16
I32 = mybir.dt.int32
AOT = mybir.AluOpType
AF = mybir.ActivationFunctionType
AXX = mybir.AxisListType.X

B, S, D = 8, 4096, 1024
H, DH, HALF = 16, 64, 32
E, CAP = 8, 4096
T = B * S
FF = 4096
NITER = 30
BIGNEG = 1.0e6
EPS = 1e-5

_DEBUG = False


class Sem:
    def __init__(self, stk, nc, name):
        self.h = stk.enter_context(nc.semaphore(name))
        self.n = 0

    def bump(self, k=1):
        self.n += k
        return self.n


class _MMIns:

    def __init__(self, eng, semh, n):
        self._e = eng
        self._s = semh
        self._n = n

    def then_inc(self, sem, k=1):
        self._e.wait_ge(self._s, self._n)
        return self._e.nop().then_inc(sem, k)


class PEGuard:

    def __init__(self, eng, semh):
        object.__setattr__(self, "_e", eng)
        object.__setattr__(self, "_s", semh)
        object.__setattr__(self, "_n", 0)

    def __getattr__(self, a):
        return getattr(self._e, a)

    def _guard(self, ins):
        object.__setattr__(self, "_n", self._n + 1)
        ins.then_inc(self._s, 1)
        return _MMIns(self._e, self._s, self._n)

    def mm_barrier(self):
        if self._n > 0:
            self._e.wait_ge(self._s, self._n)

    def matmul(self, *a, **k):
        if self._n >= 2:
            self._e.wait_ge(self._s, self._n - 1)
        return self._guard(self._e.matmul(*a, **k))

    def transpose(self, *a, **k):
        if self._n >= 2:
            self._e.wait_ge(self._s, self._n - 1)
        return self._guard(self._e.transpose(*a, **k))


class Prog:
    def __init__(self):
        self.sched = []
        self.trace = []
        self.hz = False
        self.npe = 0
        self._jV = None
        self._jS = None

    def em(self, eng, fn):
        self.sched.append((eng, fn))

    def wait(self, eng, sem, val):
        if val is None:
            return
        self.trace.append((eng, "wait", sem.h.name, val))
        self.em(eng, lambda e, s=sem.h, v=val: e.wait_ge(s, v))

    def waitt(self, eng, tok):
        if tok is None:
            return
        self.wait(eng, tok[0], tok[1])

    def settle(self, eng, sem):
        if sem.n:
            self.wait(eng, sem, sem.n)

    def dma(self, eng, sem, out, in_, **kw):
        v = sem.bump(16)
        self.trace.append((eng, "inc", sem.h.name, 16))
        def f(e, out=out, in_=in_, kw=kw, s=sem.h):
            e.dma_start(out=out, in_=in_, **kw).then_inc(s, 16)
        self.em(eng, f)
        return (sem, v)

    def pe(self, builder, sem=None):
        self.op("tensor", builder, sem)
        self.npe += 1
        self.trace.append(("tensor", "inc", "sMM", 1))
        return self.npe

    def op(self, eng, builder, sem=None, k=1):
        if sem is None:
            self.em(eng, lambda e, b=builder: b(e))
            v = None
        else:
            v = sem.bump(k)
            self.trace.append((eng, "inc", sem.h.name, k))
            self.em(eng, lambda e, b=builder, s=sem.h, kk=k: b(e).then_inc(s, kk))
        if self.hz and eng == "vector" and self._jV is not None:
            j = self._jV
            self.em(eng, lambda e, j=j: e.tensor_copy(j[:], j[:]))
        if self.hz and eng == "scalar" and self._jS is not None:
            j = self._jS
            self.em(eng, lambda e, j=j: e.activation(out=j[:], in_=j[:],
                                                     func=AF.Copy))
        return v

    def replay(self, eng_name, eng):
        for name, fn in self.sched:
            if name == eng_name:
                fn(eng)

    def check_deadlock(self):
        from collections import defaultdict
        streams = defaultdict(list)
        for ev in self.trace:
            streams[ev[0]].append(ev)
        pc = {k: 0 for k in streams}
        sems = defaultdict(int)
        progress = True
        while progress:
            progress = False
            for eng, st in streams.items():
                while pc[eng] < len(st):
                    _, kind, s, v = st[pc[eng]]
                    if kind == "inc":
                        sems[s] += v
                        pc[eng] += 1
                        progress = True
                    elif sems[s] >= v:
                        pc[eng] += 1
                        progress = True
                    else:
                        break
        stuck = {k: (pc[k], len(v)) for k, v in streams.items() if pc[k] < len(v)}
        if stuck:
            detail = {e: streams[e][pc[e]] for e in stuck}
            raise RuntimeError(f"DEADLOCK: {stuck} {detail}")


class _Stop(Exception):
    pass


def build_program(debug=False, phases=9, race=False):
    nc = bass.Bass("TRN2", target_bir_lowering=False,
                   detect_race_conditions=race)
    stk = ExitStack()

    def dinp(name, shape, dt):
        return nc.dram_tensor(name, list(shape), dt, kind="ExternalInput").ap()

    x_in = dinp("x", [S, D], F32)
    wqT = dinp("wqT", [D, D], F32)
    wkT = dinp("wkT", [D, D], F32)
    wvT = dinp("wvT", [D, D], F32)
    woT = dinp("woT", [D, D], F32)
    centT = dinp("centT", [D, E], F32)
    w1e = dinp("w1e", [D, FF], F32)
    w2e = dinp("w2e", [FF, D], F32)
    b1t = dinp("b1t", [128, FF // 128], F32)
    b2t = dinp("b2t", [128, D // 128], F32)
    g_in = dinp("g_rep", [128, D], F32)
    b_in = dinp("b_rep", [128, D], F32)
    cos_in = dinp("cos_in", [S, HALF], F32)
    sin_in = dinp("sin_in", [S, HALF], F32)
    ident_in = dinp("ident", [128, 128], F32)
    utfull_in = dinp("utfull", [128, 128], F32)
    utown_in = dinp("utown", [128, 128], F32)
    oind_in = dinp("oind", [128, 8], F32)
    o8t_in = dinp("o8t", [8, 128], F32)
    u8_in = dinp("u8", [8, 8], F32)
    onesf_in = dinp("onesf", [128, 128], F32)
    m0_in = dinp("m0", [2, 128], F32)
    m1_in = dinp("m1", [2, 128], F32)
    iota_in = dinp("iotaj", [128, 32], F32)
    masks_in = dinp("masks", [128, 4 * 512], F32)
    own32_in = dinp("own32", [32, 1], I32)
    rowsn_in = dinp("rowsn", [8, 1], I32)
    rowsb2_in = dinp("rowsb2", [8, 1], I32)
    rowsnb_in = dinp("rowsnb", [8, 1], I32)
    e4096_in = dinp("e4096", [128, 8], F32)

    out_t = nc.dram_tensor("out", [S, D], F32, kind="ExternalOutput").ap()
    dbg = {}
    if debug:
        for nm, shape in [("d_feat", [S, D]), ("d_pos", [128, 256]),
                          ("d_soffs", [128, 256]), ("d_ntab", [8, 8]),
                          ("d_idx", [128, 32]), ("d_scores", [E, S]),
                          ("d_y", [S, D]), ("d_norm", [S, D]),
                          ("d_slab", [CAP, D]), ("d_h", [CAP, D]),
                          ("d_agsc", [E * B, S])]:
            dbg[nm] = nc.dram_tensor(nm, shape, F32, kind="ExternalOutput").ap()

    qtm_d = nc.dram_tensor("qtm_d", [S, D], BF16, kind="Internal").ap()
    vtm_d = nc.dram_tensor("vtm_d", [S, D], BF16, kind="Internal").ap()
    y_d = nc.dram_tensor("y_d", [S, D], BF16, kind="Internal").ap()
    feat_d = nc.dram_tensor("feat_d", [S, D], F32, kind="Internal").ap()
    norm_d = nc.dram_tensor("norm_d", [S, D], BF16, kind="Internal").ap()
    slab_d = nc.dram_tensor("slab_d", [CAP, D], BF16, kind="Internal").ap()
    slabT_d = nc.dram_tensor("slabT_d", [128, 8 * CAP], BF16, kind="Internal").ap()
    pos_d = nc.dram_tensor("pos_d", [256, 128], F32, kind="Internal").ap()
    soffs_d = nc.dram_tensor("soffs_d", [256, 128], F32, kind="Internal").ap()
    tabs_d = nc.dram_tensor("tabs_d", [24, 128], F32, kind="Internal").ap()
    ag_sc_in = nc.dram_tensor("ag_sc_in", [E, S], F32, kind="Internal")
    ag_sc_out = nc.dram_tensor("ag_sc_out", [E * B, S], F32, kind="Internal")
    ag_nm_in = nc.dram_tensor("ag_nm_in", [S, D], BF16, kind="Internal")
    ag_nm_out = nc.dram_tensor("ag_nm_out", [T, D], BF16, kind="Internal")
    ag_h_in = nc.dram_tensor("ag_h_in", [CAP, D], BF16, kind="Internal")
    ag_h_out = nc.dram_tensor("ag_h_out", [T, D], BF16, kind="Internal")

    P = Prog()
    all_dma_sems = []

    def DS(name):
        s = Sem(stk, nc, name)
        all_dma_sems.append(s)
        return s

    def DS2(name):
        return [DS(name + "a"), DS(name + "b")]

    cc = Sem(stk, nc, "cc")
    sP = Sem(stk, nc, "sP")
    sA = Sem(stk, nc, "sA")
    sV = Sem(stk, nc, "sV")
    sMM = Sem(stk, nc, "sMM")

    cCONST = DS("cCONST")
    cCS = DS("cCS")
    cWPR = DS("cWPR")
    pX = DS2("pX")
    pXB = DS2("pXB")
    cXT = DS("cXT")
    pST = DS2("pST")
    cKT = DS("cKT")
    cVA = DS("cVA")
    cWO = DS("cWO")
    pQT = DS2("pQT")
    pYT = DS2("pYT")
    pX2 = DS2("pX2")
    pYTM = DS2("pYTM")
    pFN = DS2("pFN")
    cCENT = DS("cCENT")
    cNTD = DS("cNTD")
    cSCT = DS("cSCT")
    cSCL = DS("cSCL")
    cTAB = DS("cTAB")
    cMYT = DS("cMYT")
    cPOS = DS("cPOS")
    cOWNG = DS("cOWNG")
    pNSC = DS2("pNSC")
    pSCA = DS2("pSCA")
    cW12 = DS("cW12")
    pSGB = DS2("pSGB")
    pSLB = DS2("pSLB")
    pNT = DS2("pNT")
    pH2T = DS2("pH2T")
    cHRT = DS("cHRT")
    cHRS = DS("cHRS")
    pHB = DS2("pHB")
    pFB = DS2("pFB")
    pOUT = DS2("pOUT")
    cDBG = DS("cDBG")

    def sbt(st, name, shape, dt=F32):
        return st.enter_context(nc.sbuf_tensor(name, shape, dt))

    def pst(st, name, shape, dt=F32):
        return st.enter_context(nc.psum_tensor(name, shape, dt))

    _open_stacks = []

    def mkstack():
        st = ExitStack()
        _open_stacks.append(st)
        real_close = st.close
        def close():
            _open_stacks.remove(st)
            real_close()
        st.close = close
        return st

    ident = sbt(stk, "ident_s", [128, 128])
    utfull = sbt(stk, "utfull_s", [128, 128])
    utown = sbt(stk, "utown_s", [128, 128])
    oind = sbt(stk, "oind_s", [128, 8])
    o8t = sbt(stk, "o8t_s", [8, 128])
    u8 = sbt(stk, "u8_s", [8, 8])
    onesf = sbt(stk, "onesf_s", [128, 128])
    m0 = sbt(stk, "m0_s", [2, 128])
    m1 = sbt(stk, "m1_s", [2, 128])
    iotaj = sbt(stk, "iotaj_s", [128, 32])
    masksb = sbt(stk, "masksb", [128, 4 * 512], BF16)
    onescol = sbt(stk, "onescol", [128, 1], BF16)
    onescolf = sbt(stk, "onescolf", [128, 1])
    zeros65 = sbt(stk, "zeros65", [128, 65], BF16)
    ones1x64 = sbt(stk, "ones1x64", [1, 64])
    b1s = sbt(stk, "b1s", [128, FF // 128])
    b2s = sbt(stk, "b2s", [128, D // 128])
    own32 = sbt(stk, "own32_s", [32, 1], I32)
    rowsn = sbt(stk, "rowsn_s", [8, 1], I32)
    rowsb2 = sbt(stk, "rowsb2_s", [8, 1], I32)
    rowsnb = sbt(stk, "rowsnb_s", [8, 1], I32)
    e4096 = sbt(stk, "e4096_s", [128, 8])
    seps = sbt(stk, "seps", [128, 1])
    seps2 = sbt(stk, "seps2", [128, 1])
    junkV = sbt(stk, "junkV", [128, 1])
    junkS = sbt(stk, "junkS", [128, 1])
    idx_i = sbt(stk, "idx_i", [128, 32], I32)
    sidx_own = sbt(stk, "sidx_own", [128, 32], I32)
    spos_own = sbt(stk, "spos_own", [128, 32], I32)

    stCst = ExitStack()
    masksf = sbt(stCst, "masksf", [128, 4 * 512])
    for ap_in, ap_sb in [(ident_in, ident), (utfull_in, utfull), (utown_in, utown),
                         (oind_in, oind), (o8t_in, o8t), (u8_in, u8),
                         (onesf_in, onesf), (m0_in, m0), (m1_in, m1),
                         (iota_in, iotaj), (masks_in, masksf),
                         (b1t, b1s), (b2t, b2s), (own32_in, own32),
                         (rowsn_in, rowsn), (rowsb2_in, rowsb2),
                         (rowsnb_in, rowsnb), (e4096_in, e4096)]:
        P.dma("sync", cCONST, ap_sb[:], ap_in)
    bregS = nc.gpsimd.alloc_register("bndS")
    bregT = nc.gpsimd.alloc_register("bndT")
    P.op("gpsimd", lambda e: e.reg_mov(bregS, S - 1))
    P.op("gpsimd", lambda e: e.reg_mov(bregT, T - 1))
    P.settle("vector", cCONST)
    P.op("vector", lambda e: e.tensor_copy(masksb[:], masksf[:]))
    P.op("vector", lambda e: e.memset(onescol[:], 1.0))
    P.op("vector", lambda e: e.memset(onescolf[:], 1.0))
    P.op("vector", lambda e: e.memset(zeros65[:], 0.0))
    P.op("vector", lambda e: e.memset(ones1x64[:], 1.0))
    P.op("vector", lambda e: e.memset(seps2[:], EPS * EPS))
    v_consts = P.op("vector", lambda e: e.memset(seps[:], EPS), sV)
    P.op("vector", lambda e: e.memset(junkV[:], 0.0))
    P.op("vector", lambda e: e.memset(junkS[:], 0.0))
    P._jV, P._jS = junkV, junkS

    import os as _os
    _FM = int(_os.environ.get("FMODE", "3"))

    def phase_fence():
        if _FM & 1:
            vV = P.op("vector", lambda e: e.tensor_copy(junkV[:], junkV[:]), sV)
            vS = P.op("scalar", lambda e: e.activation(out=junkS[:], in_=junkS[:],
                                                       func=AF.Copy), sA)
            vT = P.op("tensor", lambda e: e.mm_barrier() or e.nop(), sP)
        for eng in ("sync", "scalar", "vector", "tensor", "gpsimd"):
            if _FM & 2:
                for s_ in all_dma_sems:
                    P.settle(eng, s_)
            if _FM & 1:
                P.wait(eng, sV, vV)
                P.wait(eng, sA, vS)
                P.wait(eng, sP, vT)
    stCst.close()

    try:
        stKT = mkstack()
        kT = sbt(stKT, "kT", [128, 8 * S], BF16)
        stB = mkstack()
        cosb = sbt(stB, "cosb", [128, 32 * HALF])
        sinb = sbt(stB, "sinb", [128, 32 * HALF])
        P.dma("sync", cCS, cosb[:], cos_in.rearrange("(c p) m -> p c m", p=128))
        P.dma("sync", cCS, sinb[:], sin_in.rearrange("(c p) m -> p c m", p=128))
        wq_sb = sbt(stB, "wq_sb", [128, 8 * D], BF16)
        wk_sb = sbt(stB, "wk_sb", [128, 8 * D], BF16)
        wv_sb = sbt(stB, "wv_sb", [128, 8 * D], BF16)
        for w_in, w_sb in [(wqT, wq_sb), (wkT, wk_sb), (wvT, wv_sb)]:
            P.dma("gpsimd", cWPR, w_sb[:], w_in.rearrange("(c p) i -> p c i", p=128))

        xin = [sbt(stB, f"xin{i}", [128, D]) for i in range(4)]
        xbc = [sbt(stB, f"xbc{i}", [128, D], BF16) for i in range(4)]
        xTc = [sbt(stB, f"xTc{i}", [128, 8 * 128], BF16) for i in range(2)]
        identb = sbt(stB, "identb", [128, 128], BF16)
        P.wait("vector", sV, v_consts)
        v_identb = P.op("vector", lambda e: e.tensor_copy(identb[:], ident[:]), sV)

        stQ = mkstack()
        pq = pst(stQ, "pq", [128, 1024])
        pk = pst(stQ, "pk", [128, 1024])
        pv = pst(stQ, "pv", [128, 1024])
        ptx = [pst(stQ, f"ptx{i}", [128, 128], BF16) for i in range(2)]
        qf = sbt(stB, "qf", [128, D])
        kf = sbt(stB, "kf", [128, D])
        qr = [sbt(stB, f"qr{i}", [128, D], BF16) for i in range(2)]
        kr = [sbt(stB, f"kr{i}", [128, D], BF16) for i in range(2)]
        vtm = [sbt(stB, f"vtm{i}", [128, D], BF16) for i in range(2)]
        t1 = sbt(stB, "t1", [128, HALF * H])
        t2 = sbt(stB, "t2", [128, HALF * H])
        t1v = t1[:].rearrange("p (h m one) -> p h m one", h=H, one=1)
        t2v = t2[:].rearrange("p (h m one) -> p h m one", h=H, one=1)

        P.settle("tensor", cWPR)
        P.settle("vector", cCS)
        P.wait("tensor", sV, v_identb)

        def rope_views(buf):
            v4 = buf[:].rearrange("p (h m t) -> p h m t", h=H, m=HALF, t=2)
            return v4[:, :, :, 0:1], v4[:, :, :, 1:2]

        v_evac = None
        v_rope = None
        pend_ktr = []
        pX4 = [pX[0], pX[1], pXB[0], pXB[1]]
        v_xtc = [None, None, None, None]
        v_ptx_ev = [None, None]
        v_kr_free = [None, None]
        t_qst = [None, None]
        t_vst = [None, None]
        t_vst_hist = []
        for tc in range(32):
            bx = tc % 2
            b4 = tc % 4
            P.waitt("sync", t_vst_hist[tc - 4] if tc >= 4 else None)
            t_x = P.dma("sync", pX4[b4], xin[b4][:],
                        x_in[tc * 128:(tc + 1) * 128, :])
            P.waitt("scalar", t_x)
            P.wait("scalar", sMM, v_xtc[b4])
            vv = P.op("scalar", lambda e, b=b4: e.activation(
                out=xbc[b][:], in_=xin[b][:], func=AF.Copy), sA)
            P.wait("tensor", sA, vv)
            for dc in range(8):
                pb = dc % 2
                P.wait("tensor", sV, v_ptx_ev[pb])
                ntr = P.pe(
                     lambda e, b=b4, dc=dc, pb=pb: e.transpose(
                         out=ptx[pb][:], in_=xbc[b][:, dc * 128:(dc + 1) * 128],
                         identity=identb[:]))
                P.wait("vector", sMM, ntr)
                v_ptx_ev[pb] = P.op(
                    "vector",
                    lambda e, b=bx, dc=dc, pb=pb: e.tensor_copy(
                        xTc[b][:, dc * 128:(dc + 1) * 128], ptx[pb][:]), sV)
            v_xtc[b4] = ntr
            P.wait("tensor", sV, v_ptx_ev[1])
            P.wait("tensor", sA, v_evac)
            for dc in range(8):
                lhs = xTc[bx][:, dc * 128:(dc + 1) * 128]
                for (w_sb, pp) in [(wq_sb, pq), (wk_sb, pk), (wv_sb, pv)]:
                    for ih in range(2):
                        rhs = w_sb[:, dc * D + ih * 512: dc * D + ih * 512 + 512]
                        P.pe(
                             lambda e, o=pp[:, ih * 512:ih * 512 + 512], l=lhs, r=rhs,
                                    s0=(dc == 0), s1=(dc == 7):
                             e.matmul(out=o, lhsT=l, rhs=r, start=s0, stop=s1))
            v_mm = P.npe
            while pend_ktr:
                pend_ktr.pop(0)()
            P.wait("scalar", sMM, v_mm)
            P.wait("scalar", sV, v_rope)
            P.waitt("scalar", t_vst[bx])
            P.op("scalar", lambda e: e.activation(out=qf[:], in_=pq[:], func=AF.Copy))
            P.op("scalar", lambda e: e.activation(out=kf[:], in_=pk[:], func=AF.Copy))
            v_evac = P.op("scalar", lambda e, b=bx: e.activation(
                out=vtm[b][:], in_=pv[:], func=AF.Copy), sA)
            P.wait("vector", sA, v_evac)
            P.waitt("vector", t_qst[bx])
            P.wait("vector", sMM, v_kr_free[bx])
            cosv = cosb[:, tc * HALF:(tc + 1) * HALF] \
                .rearrange("p (o1 m o2) -> p o1 m o2", o1=1, o2=1) \
                .to_broadcast([128, H, HALF, 1])
            sinv = sinb[:, tc * HALF:(tc + 1) * HALF] \
                .rearrange("p (o1 m o2) -> p o1 m o2", o1=1, o2=1) \
                .to_broadcast([128, H, HALF, 1])
            for (src, dst) in [(qf, qr[bx]), (kf, kr[bx])]:
                se, so = rope_views(src)
                de, do = rope_views(dst)
                P.op("vector", lambda e, a=se, c=cosv: e.tensor_tensor(t1v, a, c,
                                                                       AOT.mult))
                P.op("vector", lambda e, a=so, s=sinv: e.tensor_tensor(t2v, a, s,
                                                                       AOT.mult))
                P.op("vector", lambda e, d=de: e.tensor_tensor(d, t1v, t2v,
                                                               AOT.subtract))
                P.op("vector", lambda e, a=se, s=sinv: e.tensor_tensor(t1v, a, s,
                                                                       AOT.mult))
                P.op("vector", lambda e, a=so, c=cosv: e.tensor_tensor(t2v, a, c,
                                                                       AOT.mult))
                v_rope = P.op("vector", lambda e, d=do: e.tensor_tensor(d, t1v, t2v,
                                                                        AOT.add), sV)
            def emit_ktr(tc=tc, bx=bx, v_rope=v_rope):
                P.wait("tensor", sV, v_rope)
                for dc in range(8):
                    pb = dc % 2
                    P.wait("tensor", sV, v_ptx_ev[pb])
                    nktr = P.pe(
                         lambda e, b=bx, dc=dc, pb=pb: e.transpose(
                             out=ptx[pb][:],
                             in_=kr[b][:, dc * 128:(dc + 1) * 128],
                             identity=identb[:]))
                    P.wait("vector", sMM, nktr)
                    v_ptx_ev[pb] = P.op(
                        "vector",
                        lambda e, dc=dc, tc=tc, pb=pb: e.tensor_copy(
                            kT[:, dc * S + tc * 128: dc * S + (tc + 1) * 128],
                            ptx[pb][:]), sV)
                v_kr_free[bx] = nktr
            pend_ktr.append(emit_ktr)
            P.wait("gpsimd", sV, v_rope)
            bst = pST[bx]
            t_qst[bx] = P.dma("gpsimd", bst, qtm_d[tc * 128:(tc + 1) * 128, :],
                              qr[bx][:])
            t_vst[bx] = P.dma("gpsimd", bst, vtm_d[tc * 128:(tc + 1) * 128, :],
                              vtm[bx][:])
            t_vst_hist.append(t_vst[bx])
        while pend_ktr:
            pend_ktr.pop(0)()
        stQ.close()
        stB.close()
        if phases <= 2:
            raise _Stop()

        phase_fence()
        stC = mkstack()
        vaug2 = [sbt(stC, f"vaug2{i}", [128, 32 * 2 * 65], BF16) for i in range(2)]
        wo_sb = sbt(stC, "wo_sb", [128, 8 * D], BF16)
        P.dma("gpsimd", cWO, wo_sb[:], woT.rearrange("(c p) i -> p c i", p=128))
        P.settle("sync", pST[0])
        P.settle("sync", pST[1])
        P.settle("scalar", pST[0])
        P.settle("scalar", pST[1])
        v_ones = None
        for b in range(2):
            vc_ = vaug2[b][:].rearrange("p (j h c) -> p (j h) c", j=32, h=2, c=65)
            v_ones = P.op("vector", lambda e, v=vc_: e.memset(v[:, :, 64:65], 1.0),
                          sV)
        pVA = [cHRT, cHRS]

        def load_vaug(hp, b):
            tok = None
            for h_ in range(2):
                srcv = vtm_d[:, hp * 128 + h_ * 64: hp * 128 + h_ * 64 + 64] \
                    .rearrange("(a p) d -> p a d", p=128)
                dstv = vaug2[b][:].rearrange("p (j h c) -> p h j c", j=32, h=2,
                                             c=65)[:, h_, :, 0:64]
                tok = P.dma("scalar", pVA[b], dstv, srcv)
            return tok

        qTit = [sbt(stC, f"qTit{i}", [128, 8 * 512], BF16) for i in range(2)]
        expb = [[sbt(stC, f"exp{hh}{i}", [128, 512], BF16) for i in range(2)]
                for hh in range(2)]
        mbuf = [[sbt(stC, f"mbuf{hh}{i}", [128, 512], BF16) for i in range(2)]
                for hh in range(2)]
        ebf = [[sbt(stC, f"ebf{hh}{i}", [128, 512]) for i in range(2)]
               for hh in range(2)]
        accE = [sbt(stC, f"accE{hh}", [128, 512]) for hh in range(2)]
        masksC = sbt(stC, "masksC", [128, 4 * 512])
        v_mkc = P.op("vector", lambda e: e.tensor_copy(masksC[:], masksb[:]), sV)
        P.wait("scalar", sV, v_mkc)
        attnT = sbt(stC, "attnT", [128, 8 * 512], BF16)
        rden = sbt(stC, "rden", [1, 1024])
        rdenb = sbt(stC, "rdenb", [64, 2 * 512])
        ytmp = [sbt(stC, f"ytmp{i}", [128, 512], BF16) for i in range(2)]
        ppS2 = [[pst(stC, f"ppS{hh}{pp}", [128, 512]) for pp in range(2)]
                for hh in range(2)]
        pAV = [pst(stC, f"pAV{hh}", [65, 512]) for hh in range(2)]
        pden = pst(stC, "pden", [64, 512])
        pmisc = pst(stC, "pmisc", [128, 512])

        P.settle("tensor", cWO)
        P.wait("tensor", sV, v_ones)
        P.wait("scalar", sV, v_ones)
        t_va = [None, None]
        t_va[0] = load_vaug(0, 0)
        t_va[1] = load_vaug(1, 1)

        for dc in range(8):
            P.dma("sync", pQT[0], qTit[0][:, dc * 512:(dc + 1) * 512],
                  qtm_d[0:512, dc * 128:(dc + 1) * 128], transpose=True)

        v_attnT_w = None
        v_oproj_mm_end = None
        t_ytst = [None, None]

        for it in range(8):
            P.wait("tensor", pQT[it % 2], 16 * 8 * (it // 2 + 1))
            qTb = qTit[it % 2]
            first_scb = [None]
            for hp in range(8):
                vb = hp % 2
                hpi = it * 8 + hp
                P.waitt("tensor", t_va[vb])
                vstat = vaug2[vb]
                if hp == 1 and it < 7:
                    P.wait("sync", sMM, first_scb[0])
                    for dc in range(8):
                        P.dma("sync", pQT[(it + 1) % 2],
                              qTit[(it + 1) % 2][:, dc * 512:(dc + 1) * 512],
                              qtm_d[(it + 1) * 512:(it + 2) * 512,
                                    dc * 128:(dc + 1) * 128], transpose=True)
                av_first = [True, True]
                if it == 0:
                    for half in (0, 1):
                        P.pe(
                             lambda e, o=pAV[half]:
                             e.matmul(out=o[0:65, :], lhsT=zeros65[:],
                                      rhs=masksb[:, 0:512], start=True, stop=False))
                        av_first[half] = False
                v_scb = {}
                v_exp = {}
                v_ebr = {}
                first_acc = [True, True]
                v_accl = [None, None]

                def sc_round(jt, hp=hp, qTb=qTb, it=it):
                    hi_t = jt >= 4 * it
                    for half in (0, 1):
                        P.wait("tensor", sA, v_exp.get((jt - 2, half)))
                        lhs = kT[64 * half:64 * half + 64,
                                 hp * S + jt * 128: hp * S + (jt + 1) * 128]
                        rhs = qTb[64 * half:64 * half + 64,
                                  hp * 512:(hp + 1) * 512]
                        nsc = P.pe(
                             lambda e, o=ppS2[jt % 2][half], l=lhs, r=rhs,
                                    tp=(64 * half, 0):
                             e.matmul(out=o[:], lhsT=l, rhs=r, start=True,
                                      stop=True, tile_position=tp))
                    v_scb[jt] = nsc
                    if first_scb[0] is None:
                        first_scb[0] = v_scb[jt]
                    for half in (0, 1):
                        P.wait("scalar", sMM, v_scb[jt])
                        if hi_t:
                            P.wait("scalar", sV, v_ebr.get((jt - 2, half)))
                        dst = ebf[half][jt % 2] if hi_t else expb[half][jt % 2]
                        v_exp[(jt, half)] = P.op(
                            "scalar",
                            lambda e, o=dst, i=ppS2[jt % 2][half]:
                            e.activation(out=o[:], in_=i[:], func=AF.Exp,
                                         scale=0.125), sA)

                def av_round(jt, hp=hp, it=it):
                    lo_t = jt <= 4 * it - 1
                    di_t = (4 * it <= jt <= 4 * it + 3)
                    for half in (0, 1):
                        eb = expb[half][jt % 2]
                        P.wait("tensor", sA, v_exp[(jt, half)])
                        stat_v1 = vstat[:, (jt * 2 + half) * 65:
                                        (jt * 2 + half) * 65 + 65]
                        stat_v = vstat[:, (jt * 2 + half) * 65:
                                       (jt * 2 + half) * 65 + 64]
                        if lo_t:
                            P.pe(
                                 lambda e, o=pAV[half], l=stat_v1, r=eb,
                                        s0=av_first[half]:
                                 e.matmul(out=o[:], lhsT=l, rhs=r[:], start=s0,
                                          stop=False))
                            av_first[half] = False
                        elif di_t:
                            fb = ebf[half][jt % 2]
                            kk = jt - 4 * it
                            P.wait("vector", sA, v_exp[(jt, half)])
                            P.wait("vector", sMM, v_scb[jt])
                            vm = P.op("vector",
                                      lambda e, o=mbuf[half][jt % 2], a=fb,
                                             m=masksC[:, kk * 512:(kk + 1) * 512]:
                                      e.tensor_tensor(o[:], a[:], m, AOT.mult), sV)
                            if first_acc[half]:
                                vacc = P.op("vector",
                                            lambda e, a=accE[half], f=fb:
                                            e.tensor_copy(a[:], f[:]), sV)
                                first_acc[half] = False
                            else:
                                vacc = P.op("vector",
                                            lambda e, a=accE[half], f=fb:
                                            e.tensor_tensor(a[:], a[:], f[:],
                                                            AOT.add), sV)
                            v_ebr[(jt, half)] = vacc
                            v_accl[half] = vacc
                            P.wait("tensor", sV, vm)
                            P.pe(
                                 lambda e, o=pAV[half], l=stat_v,
                                        r=mbuf[half][jt % 2], s0=av_first[half]:
                                 e.matmul(out=o[0:64, :], lhsT=l, rhs=r[:],
                                          start=s0, stop=False))
                            av_first[half] = False
                        else:
                            fb = ebf[half][jt % 2]
                            P.wait("vector", sA, v_exp[(jt, half)])
                            if first_acc[half]:
                                vacc = P.op("vector",
                                            lambda e, a=accE[half], f=fb:
                                            e.tensor_copy(a[:], f[:]), sV)
                                first_acc[half] = False
                            else:
                                vacc = P.op("vector",
                                            lambda e, a=accE[half], f=fb:
                                            e.tensor_tensor(a[:], a[:], f[:],
                                                            AOT.add), sV)
                            v_ebr[(jt, half)] = vacc
                            v_accl[half] = vacc

                for jt in range(32):
                    sc_round(jt)
                    if jt >= 1:
                        av_round(jt - 1)
                av_round(31)
                for half in (0, 1):
                    if not first_acc[half]:
                        P.wait("tensor", sV, v_accl[half])
                        P.pe(
                             lambda e, o=pAV[half], r=accE[half]:
                             e.matmul(out=o[64:65, :], lhsT=onescolf[:], rhs=r[:],
                                      start=False, stop=False,
                                      tile_position=(0, 64)))
                for half in (0, 1):
                    P.pe(
                         lambda e, o=pAV[half]:
                         e.matmul(out=o[0:65, :], lhsT=zeros65[:],
                                  rhs=masksb[:, 0:512], start=False, stop=True))
                v_av = P.npe
                if hpi < 62:
                    P.wait("scalar", sMM, v_av)
                    t_va[vb] = load_vaug((hpi + 2) % 8, vb)
                v_den_evac = None
                for half in (0, 1):
                    P.wait("vector", sMM, v_av)
                    vr = P.op("vector",
                              lambda e, o=rden[0:1, half * 512:(half + 1) * 512],
                                     i=pAV[half][64:65, :]:
                              e.reciprocal(o, i), sV)
                    P.wait("tensor", sV, vr)
                    P.wait("tensor", sA, v_den_evac)
                    P.pe(
                         lambda e, r=rden[0:1, half * 512:(half + 1) * 512]:
                         e.matmul(out=pden[:], lhsT=ones1x64[:], rhs=r,
                                  start=True, stop=True))
                    P.wait("scalar", sMM, P.npe)
                    v_den_evac = P.op(
                        "scalar",
                        lambda e, o=rdenb[:, half * 512:(half + 1) * 512]:
                        e.activation(out=o, in_=pden[:], func=AF.Copy), sA)
                    P.wait("vector", sA, v_den_evac)
                    v_attnT_w = P.op(
                        "vector",
                        lambda e, o=attnT[64 * half:64 * half + 64,
                                          hp * 512:(hp + 1) * 512],
                               a=pAV[half][0:64, :],
                               r=rdenb[:, half * 512:(half + 1) * 512]:
                        e.tensor_tensor(o, a, r, AOT.mult), sV)
                P.wait("tensor", sV, v_attnT_w)
            if phases <= 2.8:
                continue
            v_ytev = None
            for ts in range(4):
                for ih in range(2):
                    pb = (2 * ts + ih) % 2
                    P.wait("tensor", sA, v_ytev)
                    for jc in range(8):
                        lhs = attnT[:, jc * 512 + ts * 128: jc * 512 + (ts + 1) * 128]
                        rhs = wo_sb[:, jc * D + ih * 512: jc * D + ih * 512 + 512]
                        P.pe(
                             lambda e, l=lhs, r=rhs, s0=(jc == 0), s1=(jc == 7):
                             e.matmul(out=pmisc[:], lhsT=l, rhs=r, start=s0, stop=s1))
                    P.wait("scalar", sMM, P.npe)
                    P.waitt("scalar", t_ytst[pb])
                    v_ytev = P.op("scalar",
                                  lambda e, o=ytmp[pb]: e.activation(
                                      out=o[:], in_=pmisc[:], func=AF.Copy), sA)
                    P.wait("gpsimd", sA, v_ytev)
                    t_ytst[pb] = P.dma(
                        "gpsimd", pYT[pb],
                        y_d[it * 512 + ts * 128: it * 512 + (ts + 1) * 128,
                            ih * 512:(ih + 1) * 512],
                        ytmp[pb][:])
            v_oproj_mm_end = P.op("tensor", lambda e: e.mm_barrier() or e.nop(), sP)
        stC.close()
        stKT.close()
        if phases <= 3:
            raise _Stop()

        phase_fence()
        stNT = mkstack()
        nTfull = sbt(stNT, "nTfull", [128, 8 * S], BF16)
        nTlo = sbt(stNT, "nTlo", [128, 8 * S], BF16)
        stD = mkstack()
        g_rep = sbt(stD, "g_rep_s", [128, D])
        b_rep = sbt(stD, "b_rep_s", [128, D])
        P.settle("sync", pYT[0])
        P.settle("sync", pYT[1])
        P.settle("scalar", pYT[0])
        P.settle("scalar", pYT[1])
        P.dma("sync", cCS, g_rep[:], g_in)
        P.dma("sync", cCS, b_rep[:], b_in)
        xin2 = [sbt(stD, f"xin2{i}", [128, D]) for i in range(2)]
        ytm = [sbt(stD, f"ytm{i}", [128, D], BF16) for i in range(2)]
        ytf = sbt(stD, "ytf", [128, D])
        sbuf_s = sbt(stD, "sbuf_s", [128, D])
        sqsc = sbt(stD, "sqsc", [128, D])
        ssq = sbt(stD, "ssq", [128, 1])
        ssum = sbt(stD, "ssum", [128, 1])
        smu = sbt(stD, "smu", [128, 1])
        svar = sbt(stD, "svar", [128, 1])
        ssd = sbt(stD, "ssd", [128, 1])
        srstd = sbt(stD, "srstd", [128, 1])
        ssd2 = sbt(stD, "ssd2", [128, 1])
        srstd2 = sbt(stD, "srstd2", [128, 1])
        nrm0 = sbt(stD, "nrm0", [128, D])
        featb = [sbt(stD, f"featb{i}", [128, D]) for i in range(2)]
        normb = [sbt(stD, f"normb{i}", [128, D], BF16) for i in range(2)]
        normlo = [sbt(stD, f"normlo{i}", [128, D], BF16) for i in range(2)]
        normf = sbt(stD, "normf", [128, D])
        nhif = sbt(stD, "nhif", [128, D])
        identb2 = sbt(stD, "identb2", [128, 128], BF16)
        v_idb2 = P.op("vector", lambda e: e.tensor_copy(identb2[:], ident[:]), sV)
        ptxD = [pst(stD, f"ptxD{i}", [128, 128], BF16) for i in range(2)]
        P.wait("tensor", sV, v_idb2)
        P.settle("vector", cCS)
        t_fn = []
        v_ptxD_ev = [None, None]
        v_nTtr = [None, None]
        v_nT_done = None
        P.hz = True
        for tc in range(32):
            bx = tc % 2
            if tc >= 2:
                P.waitt("sync", t_fn[tc - 2])
                P.waitt("scalar", t_fn[tc - 2])
                P.waitt("vector", t_fn[tc - 2])
                P.wait("vector", sMM, v_nTtr[bx])
            t_x = P.dma("sync", pX2[bx], xin2[bx][:],
                        x_in[tc * 128:(tc + 1) * 128, :])
            t_y = P.dma("scalar", pYTM[bx], ytm[bx][:],
                        y_d[tc * 128:(tc + 1) * 128, :])
            P.waitt("scalar", t_y)
            vyc = P.op("scalar", lambda e, b=bx: e.activation(
                out=ytf[:], in_=ytm[b][:], func=AF.Copy), sA)
            P.waitt("vector", t_x)
            P.wait("vector", sA, vyc)
            vsb = P.op("vector", lambda e, b=bx: e.tensor_tensor(
                sbuf_s[:], xin2[b][:], ytf[:], AOT.add), sV)
            P.wait("scalar", sV, vsb)
            P.op("scalar", lambda e: e.activation(
                out=sqsc[:], in_=sbuf_s[:], func=AF.Copy, accum_out=ssum[:]))
            vsq = P.op("scalar", lambda e: e.activation(
                out=sqsc[:], in_=sbuf_s[:], func=AF.Square, accum_out=ssq[:]),
                sA)
            P.wait("vector", sA, vsq)
            P.op("vector", lambda e: e.tensor_scalar_mul(smu[:], ssum[:], 1.0 / D))
            P.op("vector", lambda e: e.tensor_scalar_mul(svar[:], ssq[:], 1.0 / D))
            P.op("vector", lambda e: e.tensor_tensor(ssd[:], smu[:], smu[:],
                                                     AOT.mult))
            vv = P.op("vector", lambda e: e.tensor_tensor(svar[:], svar[:], ssd[:],
                                                          AOT.subtract), sV)
            P.wait("scalar", sV, vv)
            P.op("scalar", lambda e: e.activation(out=ssd[:], in_=svar[:],
                                                  func=AF.Sqrt, bias=seps[:]))
            vsd = P.op("scalar", lambda e: e.activation(out=ssd2[:], in_=svar[:],
                                                        func=AF.Sqrt,
                                                        scale=1.0 + EPS,
                                                        bias=seps2[:]), sA)
            P.wait("vector", sA, vsd)
            P.op("vector", lambda e: e.reciprocal(srstd[:], ssd[:]))
            P.op("vector", lambda e: e.reciprocal(srstd2[:], ssd2[:]))
            P.op("vector", lambda e, b=bx: e.tensor_scalar(
                out=featb[b][:], in0=sbuf_s[:], scalar1=smu[:], scalar2=srstd[:],
                op0=AOT.subtract, op1=AOT.mult))
            P.op("vector", lambda e: e.tensor_scalar(
                out=nrm0[:], in0=sbuf_s[:], scalar1=smu[:], scalar2=srstd2[:],
                op0=AOT.subtract, op1=AOT.mult))
            P.op("vector", lambda e: e.tensor_tensor(sqsc[:], nrm0[:],
                                                     g_rep[:], AOT.mult))
            P.op("vector", lambda e: e.tensor_tensor(normf[:], sqsc[:],
                                                     b_rep[:], AOT.add))
            vn = P.op("vector", lambda e, b=bx: e.tensor_copy(normb[b][:],
                                                              normf[:]), sV)
            P.op("vector", lambda e, b=bx: e.tensor_copy(nhif[:], normb[b][:]))
            P.op("vector", lambda e: e.tensor_tensor(sqsc[:], normf[:], nhif[:],
                                                     AOT.subtract))
            vnl = P.op("vector", lambda e, b=bx: e.tensor_copy(normlo[b][:],
                                                               sqsc[:]), sV)
            P.wait("gpsimd", sV, vn)
            P.dma("gpsimd", pFN[bx], feat_d[tc * 128:(tc + 1) * 128, :],
                  featb[bx][:])
            t_fn.append(P.dma("gpsimd", pFN[bx], norm_d[tc * 128:(tc + 1) * 128, :],
                              normb[bx][:]))
            P.wait("tensor", sV, vnl)
            for src_b, dstT in ((normb, nTfull), (normlo, nTlo)):
                for dc in range(8):
                    pb = dc % 2
                    P.wait("tensor", sV, v_ptxD_ev[pb])
                    ntr = P.pe(
                         lambda e, sb_=src_b[bx], dc=dc, pb=pb: e.transpose(
                             out=ptxD[pb][:], in_=sb_[:, dc * 128:(dc + 1) * 128],
                             identity=identb2[:]))
                    P.wait("vector", sMM, ntr)
                    v_ptxD_ev[pb] = P.op(
                        "vector",
                        lambda e, dc=dc, tc=tc, pb=pb, dT=dstT: e.tensor_copy(
                            dT[:, dc * S + tc * 128: dc * S + (tc + 1) * 128],
                            ptxD[pb][:]), sV)
            v_nTtr[bx] = ntr
            v_nT_done = v_ptxD_ev[1]
        P.hz = False
        stD.close()
        if phases <= 4:
            raise _Stop()

        phase_fence()
        stE = mkstack()
        centf = sbt(stE, "centf", [128, 8 * E])
        centfl = sbt(stE, "centfl", [128, 8 * E])
        cent_hi = sbt(stE, "cent_hi", [128, 8 * E], BF16)
        cent_lo = sbt(stE, "cent_lo", [128, 8 * E], BF16)
        P.dma("gpsimd", cCENT, centf[:],
              centT.rearrange("(c p) e -> p c e", p=128))
        scT = sbt(stE, "scT", [8, S])
        ps_sc = pst(stE, "ps_sc", [8, S])
        P.settle("vector", cCENT)
        P.hz = True
        P.op("vector", lambda e: e.tensor_copy(cent_hi[:], centf[:]))
        P.op("vector", lambda e: e.tensor_copy(centfl[:], cent_hi[:]))
        P.op("vector", lambda e: e.tensor_tensor(centfl[:], centf[:], centfl[:],
                                                 AOT.subtract))
        v_csp = P.op("vector", lambda e: e.tensor_copy(cent_lo[:], centfl[:]), sV)
        P.hz = False
        P.wait("tensor", sV, v_csp)
        P.wait("tensor", sV, v_nT_done)
        for tt in range(8):
            passes = [(cent_hi, nTfull), (cent_hi, nTlo), (cent_lo, nTfull)]
            for dc in range(8):
                for pi, (cs, nT_) in enumerate(passes):
                    P.pe(
                         lambda e, o=ps_sc[:, tt * 512:(tt + 1) * 512],
                                l=cs[:, dc * E:(dc + 1) * E],
                                r=nT_[:, dc * S + tt * 512:
                                      dc * S + (tt + 1) * 512],
                                s0=(dc == 0 and pi == 0),
                                s1=(dc == 7 and pi == 2):
                         e.matmul(out=o, lhsT=l, rhs=r, start=s0, stop=s1))
        vmm = P.op("tensor", lambda e: e.mm_barrier() or e.nop(), sP)
        P.wait("vector", sP, vmm)
        v_scT = P.op("vector", lambda e: e.tensor_copy(scT[:], ps_sc[:]), sV)
        P.wait("gpsimd", sV, v_scT)
        P.dma("gpsimd", cSCT, ag_sc_in.ap(), scT[:])
        P.settle("gpsimd", cSCT)
        if phases <= 4.6:
            raise _Stop()
        v_ag1 = P.op("gpsimd", lambda e: e.collective_compute(
            "AllGather", AOT.bypass, replica_groups=[list(range(8))],
            ins=[ag_sc_in.ap().opt()], outs=[ag_sc_out.ap().opt()]), cc)
        stE.close()
        stNT.close()
        if phases <= 5:
            raise _Stop()

        phase_fence()
        stF = mkstack()
        P.hz = True
        sc_all = sbt(stF, "sc_all", [128, E * 256])
        avail = sbt(stF, "avail", [128, 256])
        sm = sbt(stF, "sm", [128, 256])
        sm2 = sbt(stF, "sm2", [128, 256])
        tmp = sbt(stF, "tmp", [128, 256])
        onesr = sbt(stF, "onesr", [128, 256])
        mx2 = sbt(stF, "mx2", [128, 2])
        vals = sbt(stF, "vals", [2, 1])
        lo = sbt(stF, "lo", [128, 1])
        hi0 = sbt(stF, "hi0", [128, 1])
        wid = [sbt(stF, f"wid{i}", [128, 1]) for i in range(2)]
        tau = sbt(stF, "tau", [128, 1])
        cnt = sbt(stF, "cnt", [128, 1])
        pred = sbt(stF, "pred", [128, 1])
        offs = sbt(stF, "offs", [128, 1])
        sel0 = sbt(stF, "sel0", [128, 256])
        sel = sbt(stF, "sel", [128, 256])
        scan1 = sbt(stF, "scan1", [128, 256])
        gt = sbt(stF, "gt", [128, 256])
        rr = sbt(stF, "rr", [128, 256])
        basespp = sbt(stF, "basespp", [128, 1])
        nbarpp = sbt(stF, "nbarpp", [128, 1])
        soffs_all = sbt(stF, "soffs_all", [128, 256])
        pos_all = sbt(stF, "pos_all", [128, 256])
        ntab = sbt(stF, "ntab", [8, 8])
        b2tab = sbt(stF, "b2tab", [8, 8])
        nbtab = sbt(stF, "nbtab", [8, 8])
        runb = sbt(stF, "runb", [8, 1])
        runsnap = sbt(stF, "runsnap", [8, 1])
        ncol = sbt(stF, "ncol", [8, 1])
        nbcol = sbt(stF, "nbcol", [8, 1])
        pad3 = [sbt(stF, f"pad{i}", [8, 128]) for i in range(3)]
        pF_tot = pst(stF, "pF_tot", [128, 1])
        pF_t2 = pst(stF, "pF_t2", [2, 128])
        pF_pp = pst(stF, "pF_pp", [128, 1])
        pF_n8 = pst(stF, "pF_n8", [8, 1])
        pF_88 = pst(stF, "pF_88", [8, 8])
        pF_32 = pst(stF, "pF_32", [128, 32])

        P.wait("sync", cc, v_ag1)
        for e_ in range(E):
            for c_ in range(B):
                P.dma("sync", cSCL,
                      sc_all[16 * c_:16 * c_ + 16, e_ * 256:(e_ + 1) * 256],
                      ag_sc_out.ap()[8 * c_ + e_:8 * c_ + e_ + 1, :]
                      .rearrange("one (p f) -> (one p) f", p=16))
        P.settle("vector", cSCL)
        for buf, valz in [(avail, 1.0), (onesr, 1.0), (soffs_all, 0.0),
                          (pos_all, 0.0)]:
            P.op("vector", lambda e, b=buf, v=valz: e.memset(b[:], v))
        for buf in (runb, ntab, b2tab, nbtab, pad3[0], pad3[1], pad3[2]):
            P.op("vector", lambda e, b=buf: e.memset(b[:], 0.0))
        v_initF = P.op("vector", lambda e: e.memset(mx2[:], 0.0), sV)
        P.wait("tensor", sV, v_initF)

        v_snap_used = None
        for e_ in range(E):
            se = sc_all[:, e_ * 256:(e_ + 1) * 256]
            if e_ < 7:
                P.op("vector", lambda e, a=se: e.tensor_tensor(tmp[:], a, avail[:],
                                                               AOT.mult))
                P.op("vector", lambda e: e.tensor_scalar(
                    out=sm2[:], in0=avail[:], scalar1=-1.0, scalar2=BIGNEG,
                    op0=AOT.add, op1=AOT.mult))
                P.op("vector", lambda e: e.tensor_tensor(sm[:], tmp[:], sm2[:],
                                                         AOT.add))
                P.op("vector", lambda e: e.tensor_tensor(sm2[:], sm2[:], tmp[:],
                                                         AOT.subtract))
                P.op("vector", lambda e: e.reduce_max(mx2[:, 0:1], sm[:], axis=AXX))
                vmx = P.op("vector", lambda e: e.reduce_max(mx2[:, 1:2], sm2[:],
                                                            axis=AXX), sV)
                P.wait("tensor", sV, vmx)
                P.pe( lambda e: e.transpose(out=pF_t2[:], in_=mx2[:],
                                                     identity=ident[:]))
                vt2 = P.op("tensor", lambda e: e.mm_barrier() or e.nop(), sP)
                P.wait("vector", sP, vt2)
                vv2 = P.op("vector", lambda e: e.reduce_max(vals[:], pF_t2[:],
                                                            axis=AXX), sV)
                P.wait("tensor", sV, vv2)
                P.pe( lambda e: e.matmul(out=pF_pp[:], lhsT=m0[:],
                                                  rhs=vals[:], start=True,
                                                  stop=True))
                vb0 = P.op("tensor", lambda e: e.mm_barrier() or e.nop(), sP)
                P.wait("vector", sP, vb0)
                vhi = P.op("vector", lambda e: e.tensor_scalar_add(hi0[:], pF_pp[:],
                                                                   1.0), sV)
                P.wait("tensor", sV, vhi)
                P.pe( lambda e: e.matmul(out=pF_pp[:], lhsT=m1[:],
                                                  rhs=vals[:], start=True,
                                                  stop=True))
                vb1 = P.op("tensor", lambda e: e.mm_barrier() or e.nop(), sP)
                P.wait("vector", sP, vb1)
                P.op("vector", lambda e: e.tensor_scalar(
                    out=lo[:], in0=pF_pp[:], scalar1=-1.0, scalar2=-1.0,
                    op0=AOT.mult, op1=AOT.add))
                P.op("vector", lambda e: e.tensor_tensor(wid[0][:], hi0[:], lo[:],
                                                         AOT.subtract))
                P.op("vector", lambda e: e.tensor_scalar(
                    out=tau[:], in0=wid[0][:], scalar1=0.5, scalar2=None,
                    op0=AOT.mult))
                P.op("vector", lambda e: e.tensor_tensor(tau[:], tau[:], lo[:],
                                                         AOT.add))
                for i in range(NITER):
                    vc = P.op("vector", lambda e: e.tensor_scalar(
                        out=tmp[:], in0=sm[:], scalar1=tau[:], scalar2=0.0,
                        op0=AOT.is_ge, op1=AOT.add, accum_out=cnt[:]), sV)
                    P.wait("tensor", sV, vc)
                    P.pe( lambda e: e.matmul(out=pF_tot[:], lhsT=onesf[:],
                                                      rhs=cnt[:], start=True,
                                                      stop=True))
                    vt = P.op("tensor", lambda e: e.mm_barrier() or e.nop(), sP)
                    P.wait("vector", sP, vt)
                    P.op("vector", lambda e: e.tensor_scalar(
                        out=pred[:], in0=pF_tot[:], scalar1=float(CAP),
                        scalar2=None, op0=AOT.is_ge))
                    wprev, wnext = wid[i % 2], wid[(i + 1) % 2]
                    P.op("vector", lambda e, wp=wprev, wn=wnext: e.tensor_scalar(
                        out=wn[:], in0=wp[:], scalar1=0.5, scalar2=None,
                        op0=AOT.mult))
                    P.op("vector", lambda e, wn=wnext: e.scalar_tensor_tensor(
                        out=lo[:], in0=pred[:], scalar=wn[:], in1=lo[:],
                        op0=AOT.mult, op1=AOT.add))
                    P.op("vector", lambda e, wn=wnext: e.tensor_scalar(
                        out=tau[:], in0=wn[:], scalar1=0.5, scalar2=None,
                        op0=AOT.mult))
                    P.op("vector", lambda e: e.tensor_tensor(tau[:], tau[:], lo[:],
                                                             AOT.add))
                P.op("vector", lambda e: e.tensor_scalar(
                    out=sel0[:], in0=sm[:], scalar1=lo[:], scalar2=None,
                    op0=AOT.is_ge))
            else:
                P.op("vector", lambda e: e.tensor_copy(sel0[:], avail[:]))
            P.op("vector", lambda e: e.tensor_tensor_scan(
                out=scan1[:], data0=onesr[:], data1=sel0[:], initial=0.0,
                op0=AOT.mult, op1=AOT.add))
            vcp = P.op("vector", lambda e: e.tensor_copy(cnt[:], scan1[:, 255:256]),
                       sV)
            P.wait("tensor", sV, vcp)
            P.pe( lambda e: e.matmul(out=pF_pp[:], lhsT=utfull[:],
                                              rhs=cnt[:], start=True, stop=True))
            vmm = P.op("tensor", lambda e: e.mm_barrier() or e.nop(), sP)
            P.wait("vector", sP, vmm)
            P.op("vector", lambda e: e.tensor_copy(offs[:], pF_pp[:]))
            P.op("vector", lambda e: e.tensor_tensor(gt[:], scan1[:], sel0[:],
                                                     AOT.subtract))
            P.op("vector", lambda e: e.tensor_scalar(
                out=gt[:], in0=gt[:], scalar1=offs[:], scalar2=None, op0=AOT.add))
            P.op("vector", lambda e: e.tensor_scalar(
                out=tmp[:], in0=gt[:], scalar1=float(CAP), scalar2=None,
                op0=AOT.is_lt))
            P.op("vector", lambda e: e.tensor_tensor(sel[:], sel0[:], tmp[:],
                                                     AOT.mult))
            P.op("vector", lambda e: e.tensor_tensor_scan(
                out=scan1[:], data0=onesr[:], data1=sel[:], initial=0.0,
                op0=AOT.mult, op1=AOT.add))
            vcp = P.op("vector", lambda e: e.tensor_copy(cnt[:], scan1[:, 255:256]),
                       sV)
            P.wait("tensor", sV, vcp)
            P.pe( lambda e: e.matmul(out=pF_pp[:], lhsT=utown[:],
                                              rhs=cnt[:], start=True, stop=True))
            P.pe( lambda e: e.matmul(out=pF_n8[:], lhsT=oind[:],
                                              rhs=cnt[:], start=True, stop=True))
            vmn = P.op("tensor", lambda e: e.mm_barrier() or e.nop(), sP)
            P.wait("vector", sP, vmn)
            P.op("vector", lambda e: e.tensor_copy(offs[:], pF_pp[:]))
            P.op("vector", lambda e: e.tensor_tensor(rr[:], scan1[:], sel[:],
                                                     AOT.subtract))
            P.op("vector", lambda e: e.tensor_scalar(
                out=rr[:], in0=rr[:], scalar1=offs[:], scalar2=None, op0=AOT.add))
            vnc = P.op("vector", lambda e: e.tensor_copy(ncol[:], pF_n8[:]), sV)
            P.op("vector", lambda e, e_=e_: e.tensor_copy(ntab[:, e_:e_ + 1],
                                                          ncol[:]))
            P.op("vector", lambda e, e_=e_: e.tensor_copy(b2tab[:, e_:e_ + 1],
                                                          runb[:]))
            P.wait("vector", sP, v_snap_used)
            vrs = P.op("vector", lambda e: e.tensor_copy(runsnap[:], runb[:]), sV)
            P.wait("tensor", sV, vnc)
            P.wait("tensor", sV, vrs)
            P.pe( lambda e: e.matmul(out=pF_n8[:], lhsT=u8[:], rhs=ncol[:],
                                              start=True, stop=True))
            P.pe( lambda e: e.matmul(out=pF_pp[:], lhsT=o8t[:],
                                              rhs=runsnap[:], start=True,
                                              stop=True))
            vb2 = P.op("tensor", lambda e: e.mm_barrier() or e.nop(), sP)
            v_snap_used = vb2
            P.wait("vector", sP, vb2)
            P.op("vector", lambda e: e.tensor_copy(nbcol[:], pF_n8[:]))
            P.op("vector", lambda e, e_=e_: e.tensor_copy(nbtab[:, e_:e_ + 1],
                                                          nbcol[:]))
            vbp = P.op("vector", lambda e: e.tensor_copy(basespp[:], pF_pp[:]), sV)
            P.wait("tensor", sV, vbp)
            P.pe( lambda e: e.matmul(out=pF_pp[:], lhsT=o8t[:],
                                              rhs=nbcol[:], start=True, stop=True))
            vnbb = P.op("tensor", lambda e: e.mm_barrier() or e.nop(), sP)
            P.wait("vector", sP, vnbb)
            P.op("vector", lambda e: e.tensor_copy(nbarpp[:], pF_pp[:]))
            P.op("vector", lambda e: e.tensor_scalar(
                out=tmp[:], in0=rr[:], scalar1=basespp[:], scalar2=None,
                op0=AOT.add))
            P.op("vector", lambda e: e.tensor_tensor(tmp[:], tmp[:], sel[:],
                                                     AOT.mult))
            P.op("vector", lambda e: e.tensor_tensor(soffs_all[:], soffs_all[:],
                                                     tmp[:], AOT.add))
            P.op("vector", lambda e, e_=e_: e.tensor_scalar(
                out=tmp[:], in0=rr[:], scalar1=nbarpp[:], scalar2=float(CAP * e_),
                op0=AOT.add, op1=AOT.add))
            P.op("vector", lambda e: e.tensor_tensor(tmp[:], tmp[:], sel[:],
                                                     AOT.mult))
            P.op("vector", lambda e: e.tensor_tensor(pos_all[:], pos_all[:],
                                                     tmp[:], AOT.add))
            P.op("vector", lambda e: e.tensor_tensor(avail[:], avail[:], sel[:],
                                                     AOT.subtract))
            P.op("vector", lambda e: e.tensor_tensor(runb[:], runb[:], ncol[:],
                                                     AOT.add))

        vtt = P.op("vector", lambda e: e.tensor_copy(mx2[0:8, 0:2], mx2[0:8, 0:2]),
                   sV)
        vpd = None
        for i, tab in enumerate((ntab, b2tab, nbtab)):
            P.wait("tensor", sV, vtt if i == 0 else vpd)
            P.pe( lambda e, t=tab: e.transpose(out=pF_88[:], in_=t[:],
                                                        identity=ident[0:8, 0:8]))
            vtr = P.op("tensor", lambda e: e.mm_barrier() or e.nop(), sP)
            P.wait("vector", sP, vtr)
            vpd = P.op("vector", lambda e, i=i: e.tensor_copy(pad3[i][:, 0:8],
                                                              pF_88[:]), sV)
            P.wait("gpsimd", sV, vpd)
            P.dma("gpsimd", cTAB, tabs_d[i * 8:(i + 1) * 8, :], pad3[i][:])
        vps = P.op("vector", lambda e: e.tensor_copy(tmp[:], pos_all[:]), sV)
        P.wait("gpsimd", sV, vps)
        P.dma("gpsimd", cPOS, soffs_d.rearrange("(p a) c -> p (a c)", p=128),
              soffs_all[:])
        P.dma("gpsimd", cPOS, pos_d.rearrange("(p a) c -> p (a c)", p=128),
              pos_all[:])
        if debug:
            P.dma("gpsimd", cDBG, dbg["d_pos"], pos_all[:])
            P.dma("gpsimd", cDBG, dbg["d_soffs"], soffs_all[:])
            P.dma("gpsimd", cDBG, dbg["d_ntab"], ntab[:])

        mytabs = [sbt(stF, f"mytab{i}", [8, 128]) for i in range(3)]
        myb = sbt(stF, "myb", [128, 24])
        gtab = sbt(stF, "gtab", [128, 8])
        fmask = sbt(stF, "fmask", [128, 32])
        dG = sbt(stF, "dG", [128, 32])
        acc = sbt(stF, "acc", [128, 32])
        idx_f = sbt(stF, "idx_f", [128, 32])
        own_f = sbt(stF, "own_f", [32, 128])

        P.settle("gpsimd", cTAB)
        for i, rowsel in enumerate((rowsn, rowsb2, rowsnb)):
            P.op("gpsimd",
                 lambda e, i=i, r=rowsel: e.indirect_dma_start(
                     out=mytabs[i][:], out_offset=None, in_=tabs_d,
                     in_offset=IndirectOffsetOnAxis(r[:], 0)), cMYT, 16)
        P.settle("tensor", cMYT)
        for i in range(3):
            P.pe(
                 lambda e, i=i: e.matmul(out=pF_32[:, i * 8:(i + 1) * 8],
                                         lhsT=onesf[0:1, :],
                                         rhs=mytabs[i][0:1, 0:8], start=True,
                                         stop=True))
        vbc = P.op("tensor", lambda e: e.mm_barrier() or e.nop(), sP)
        P.wait("vector", sP, vbc)
        P.op("vector", lambda e: e.tensor_copy(myb[:], pF_32[:, 0:24]))
        P.op("vector", lambda e: e.tensor_tensor(gtab[:], myb[:, 8:16],
                                                 myb[:, 16:24], AOT.subtract))
        P.op("vector", lambda e: e.tensor_tensor(gtab[:], gtab[:], e4096[:],
                                                 AOT.add))
        P.op("vector", lambda e: e.tensor_scalar(
            out=acc[:], in0=iotaj[:], scalar1=0.0, scalar2=gtab[:, 0:1],
            op0=AOT.mult, op1=AOT.add))
        for c_ in range(1, 8):
            P.op("vector", lambda e, c_=c_: e.tensor_scalar(
                out=fmask[:], in0=iotaj[:], scalar1=myb[:, 16 + c_:17 + c_],
                scalar2=None, op0=AOT.is_ge))
            P.op("vector", lambda e, c_=c_: e.tensor_tensor(
                dG[:], gtab[:, c_:c_ + 1].to_broadcast([128, 32]),
                gtab[:, c_ - 1:c_].to_broadcast([128, 32]), AOT.subtract))
            P.op("vector", lambda e: e.tensor_tensor(fmask[:], fmask[:], dG[:],
                                                     AOT.mult))
            P.op("vector", lambda e: e.tensor_tensor(acc[:], acc[:], fmask[:],
                                                     AOT.add))
        P.op("vector", lambda e: e.tensor_tensor(idx_f[:], iotaj[:], acc[:],
                                                 AOT.add))
        v_idx = P.op("vector", lambda e: e.tensor_copy(idx_i[:], idx_f[:]), sV)
        if debug:
            P.wait("gpsimd", sV, v_idx)
            P.dma("gpsimd", cDBG, dbg["d_idx"], idx_f[:])

        P.settle("gpsimd", cPOS)
        v_own = None
        vprev_tr = None
        for (src_d, dst_i) in [(soffs_d, sidx_own), (pos_d, spos_own)]:
            P.wait("gpsimd", sP, vprev_tr)
            t_g = P.op("gpsimd",
                       lambda e, s=src_d: e.indirect_dma_start(
                           out=own_f[:], out_offset=None, in_=s,
                           in_offset=IndirectOffsetOnAxis(own32[:], 0)), cOWNG, 16)
            P.wait("tensor", cOWNG, t_g)
            P.wait("tensor", sV, v_own)
            P.pe( lambda e: e.transpose(out=pF_32[:], in_=own_f[:],
                                                 identity=ident[0:32, 0:32]))
            vprev_tr = P.op("tensor", lambda e: e.mm_barrier() or e.nop(), sP)
            P.wait("vector", sP, vprev_tr)
            v_own = P.op("vector", lambda e, d=dst_i: e.tensor_copy(d[:], pF_32[:]),
                         sV)
        P.hz = False
        stF.close()
        if phases <= 6:
            raise _Stop()

        phase_fence()
        stG = mkstack()
        nsc = [sbt(stG, f"nsc{i}", [128, D], BF16) for i in range(2)]
        P.settle("sync", pFN[0])
        P.settle("sync", pFN[1])
        P.wait("sync", sV, v_own)
        t_sca = [None, None]
        for k in range(32):
            bx = k % 2
            P.waitt("sync", t_sca[bx])
            t_l = P.dma("sync", pNSC[bx], nsc[bx][:],
                        norm_d[k * 128:(k + 1) * 128, :])
            P.waitt("gpsimd", t_l)
            vsc = P.op("gpsimd",
                       lambda e, b=bx, k=k: e.indirect_dma_start(
                           out=ag_nm_in.ap(),
                           out_offset=IndirectOffsetOnAxis(sidx_own[:, k:k + 1], 0),
                           in_=nsc[b][:], in_offset=None,
                           bounds_check=bregS, oob_is_err=False), pSCA[bx], 16)
            t_sca[bx] = (pSCA[bx], vsc)
        P.settle("gpsimd", pSCA[0])
        P.settle("gpsimd", pSCA[1])
        v_ag2 = P.op("gpsimd", lambda e: e.collective_compute(
            "AllGather", AOT.bypass, replica_groups=[list(range(8))],
            ins=[ag_nm_in.ap().opt()], outs=[ag_nm_out.ap().opt()]), cc)
        stG.close()

        phase_fence()
        stH = mkstack()
        w1_sb = sbt(stH, "w1_sb", [128, 8 * FF], BF16)
        w2_sb = sbt(stH, "w2_sb", [128, 32 * D], BF16)
        P.dma("gpsimd", cW12, w1_sb[:], w1e.rearrange("(c p) f -> p c f", p=128))
        P.dma("gpsimd", cW12, w2_sb[:], w2e.rearrange("(c p) d2 -> p c d2", p=128))
        sgb = [sbt(stH, f"sgb{i}", [128, D], BF16) for i in range(2)]
        stagT = [sbt(stH, f"stagT{i}", [128, 8 * 128], BF16) for i in range(2)]
        identb3 = sbt(stH, "identb3", [128, 128], BF16)
        ptxH = [pst(stH, f"ptxH{i}", [128, 128], BF16) for i in range(2)]
        v_idb3 = P.op("vector", lambda e: e.tensor_copy(identb3[:], ident[:]), sV)
        P.wait("tensor", sV, v_idb3)
        P.wait("gpsimd", cc, v_ag2)
        nT = [sbt(stH, f"nT{i}", [128, 8 * 512], BF16) for i in range(2)]
        gT = sbt(stH, "gT", [128, 32 * 512], BF16)
        h2T = sbt(stH, "h2T", [128, 8 * 512], BF16)
        hrowb = [sbt(stH, f"hrowb{i}", [128, D], BF16) for i in range(2)]
        v_sgb_free = [None, None]
        v_ptxH_ev = [None, None]
        t_stg = [None, None]
        t_sdb = [None, None]
        for k in range(32):
            bx = k % 2
            P.wait("gpsimd", sMM, v_sgb_free[bx])
            P.waitt("gpsimd", t_sdb[bx])
            vg = P.op("gpsimd",
                      lambda e, b=bx, k=k: e.indirect_dma_start(
                          out=sgb[b][:], out_offset=None, in_=ag_nm_out.ap(),
                          in_offset=IndirectOffsetOnAxis(idx_i[:, k:k + 1], 0),
                          bounds_check=bregT, oob_is_err=False), pSGB[bx], 16)
            if debug:
                P.waitt("gpsimd", (pSGB[bx], vg))
                t_sdb[bx] = P.dma("gpsimd", cHRT,
                                  slab_d[k * 128:(k + 1) * 128, :], sgb[bx][:])
            P.wait("tensor", pSGB[bx], vg)
            P.waitt("vector", t_stg[bx])
            for dc in range(8):
                pb = dc % 2
                P.wait("tensor", sV, v_ptxH_ev[pb])
                ntr = P.pe(
                     lambda e, b=bx, dc=dc, pb=pb: e.transpose(
                         out=ptxH[pb][:], in_=sgb[b][:, dc * 128:(dc + 1) * 128],
                         identity=identb3[:]))
                P.wait("vector", sMM, ntr)
                v_ptxH_ev[pb] = P.op(
                    "vector",
                    lambda e, b=bx, dc=dc, pb=pb: e.tensor_copy(
                        stagT[b][:, dc * 128:(dc + 1) * 128], ptxH[pb][:]), sV)
            v_sgb_free[bx] = ntr
            P.wait("gpsimd", sV, v_ptxH_ev[1])
            dstv = slabT_d.rearrange("p (c t) -> p c t", c=8)[:, :,
                                                             k * 128:(k + 1) * 128]
            srcv = stagT[bx][:].rearrange("p (c t) -> p c t", c=8)
            t_stg[bx] = P.dma("gpsimd", pSLB[bx], dstv, srcv)
            if k == 8:
                P.settle("sync", pSLB[0])
                P.settle("sync", pSLB[1])
                for dc in range(8):
                    P.dma("sync", pNT[0], nT[0][:, dc * 512:(dc + 1) * 512],
                          slabT_d[:, dc * CAP: dc * CAP + 512])

        pH1 = [pst(stH, f"pH1{i}", [128, 512]) for i in range(2)]
        pH2 = pst(stH, "pH2", [128, 1024])

        P.settle("sync", pSLB[0])
        P.settle("sync", pSLB[1])
        P.settle("tensor", cW12)
        v_gelu = [None, None]
        v_h2ev = None
        t_hrst = [None, None]
        v_hrow_bar = None
        pend_hrow = []
        v_mm1_end = [None, None]
        for s_ in range(8):
            if s_ < 7:
                P.wait("sync", sP, v_mm1_end[(s_ + 1) % 2])
                for dc in range(8):
                    P.dma("sync", pNT[(s_ + 1) % 2],
                          nT[(s_ + 1) % 2][:, dc * 512:(dc + 1) * 512],
                          slabT_d[:, dc * CAP + (s_ + 1) * 512:
                                  dc * CAP + (s_ + 2) * 512])
            P.wait("tensor", pNT[s_ % 2], 16 * 8 * (s_ // 2 + 1))
            nTb = nT[s_ % 2]
            for fc in range(32):
                if fc == 8:
                    while pend_hrow:
                        pend_hrow.pop(0)()
                pb = fc % 2
                P.wait("tensor", sA, v_gelu[pb])
                for dc in range(8):
                    lhs = w1_sb[:, dc * FF + fc * 128: dc * FF + (fc + 1) * 128]
                    P.pe(
                         lambda e, o=pH1[pb], l=lhs,
                                r=nTb[:, dc * 512:(dc + 1) * 512],
                                s0=(dc == 0), s1=(dc == 7):
                         e.matmul(out=o[:], lhsT=l, rhs=r, start=s0, stop=s1))
                P.wait("scalar", sMM, P.npe)
                v_gelu[pb] = P.op("scalar",
                                  lambda e, fc=fc, pb=pb: e.activation(
                                      out=gT[:, fc * 512:(fc + 1) * 512],
                                      in_=pH1[pb][:], func=AF.Gelu_apprx_tanh,
                                      bias=b1s[:, fc:fc + 1]), sA)
            v_mm1_end[s_ % 2] = P.op("tensor",
                                     lambda e: e.mm_barrier() or e.nop(), sP)
            for dc in range(8):
                hb = dc % 2
                if dc == 0:
                    P.wait("tensor", sA, v_gelu[0])
                    P.wait("tensor", sA, v_gelu[1])
                if v_h2ev is not None:
                    P.wait("tensor", sA, v_h2ev)
                for fc in range(32):
                    lhs = w2_sb[:, fc * D + dc * 128: fc * D + (dc + 1) * 128]
                    P.pe(
                         lambda e, o=pH2[:, hb * 512:(hb + 1) * 512], l=lhs,
                                r=gT[:, fc * 512:(fc + 1) * 512],
                                s0=(fc == 0), s1=(fc == 31):
                         e.matmul(out=o, lhsT=l, rhs=r, start=s0, stop=s1))
                P.wait("scalar", sMM, P.npe)
                if dc == 0:
                    P.wait("scalar", sMM, v_hrow_bar)
                v_h2ev = P.op("scalar",
                              lambda e, dc=dc, hb=hb: e.activation(
                                  out=h2T[:, dc * 512:(dc + 1) * 512],
                                  in_=pH2[:, hb * 512:(hb + 1) * 512],
                                  func=AF.Identity, bias=b2s[:, dc:dc + 1]), sA)
            def emit_hrow(s_=s_, v_h2ev=v_h2ev):
                nonlocal v_hrow_bar
                P.wait("tensor", sA, v_h2ev)
                for ts in range(4):
                    hb = ts % 2
                    P.waitt("vector", t_hrst[hb])
                    for dc in range(8):
                        pb = dc % 2
                        P.wait("tensor", sV, v_ptxH_ev[pb])
                        nhr = P.pe(
                             lambda e, ts=ts, dc=dc, pb=pb: e.transpose(
                                 out=ptxH[pb][:],
                                 in_=h2T[:, dc * 512 + ts * 128:
                                         dc * 512 + (ts + 1) * 128],
                                 identity=identb3[:]))
                        P.wait("vector", sMM, nhr)
                        v_ptxH_ev[pb] = P.op(
                            "vector",
                            lambda e, hb=hb, dc=dc, pb=pb: e.tensor_copy(
                                hrowb[hb][:, dc * 128:(dc + 1) * 128],
                                ptxH[pb][:]), sV)
                    P.wait("gpsimd", sV, v_ptxH_ev[1])
                    t_hrst[hb] = P.dma(
                        "gpsimd", pH2T[hb],
                        ag_h_in.ap()[s_ * 512 + ts * 128:
                                     s_ * 512 + (ts + 1) * 128, :],
                        hrowb[hb][:])
                v_hrow_bar = nhr
            pend_hrow.append(emit_hrow)
        while pend_hrow:
            pend_hrow.pop(0)()
        P.settle("gpsimd", pH2T[0])
        P.settle("gpsimd", pH2T[1])
        v_ag3 = P.op("gpsimd", lambda e: e.collective_compute(
            "AllGather", AOT.bypass, replica_groups=[list(range(8))],
            ins=[ag_h_in.ap().opt()], outs=[ag_h_out.ap().opt()]), cc)
        stH.close()
        if phases <= 8:
            raise _Stop()

        phase_fence()
        stI = mkstack()
        NB = 4
        hband = [sbt(stI, f"hband{i}", [128, D], BF16) for i in range(NB)]
        fband = [sbt(stI, f"fband{i}", [128, D]) for i in range(NB)]
        vband = [sbt(stI, f"vband{i}", [128, D]) for i in range(2)]
        oband = [sbt(stI, f"oband{i}", [128, D]) for i in range(NB)]
        sq2 = [sbt(stI, f"sq2{i}", [128, D]) for i in range(2)]
        hf2 = [sbt(stI, f"hf2{i}", [128, D]) for i in range(2)]
        lsum = [sbt(stI, f"lsum{i}", [128, 1]) for i in range(2)]
        lsq = [sbt(stI, f"lsq{i}", [128, 1]) for i in range(2)]
        lmu = [sbt(stI, f"lmu{i}", [128, 1]) for i in range(2)]
        lvar = [sbt(stI, f"lvar{i}", [128, 1]) for i in range(2)]
        lsd = [sbt(stI, f"lsd{i}", [128, 1]) for i in range(2)]
        lrstd = [sbt(stI, f"lrstd{i}", [128, 1]) for i in range(2)]
        P.wait("gpsimd", cc, v_ag3)
        t_out = []
        t_hb = [None] * NB
        t_fb = [None] * NB
        sHB4 = [pHB[0], pHB[1], pSGB[0], pSGB[1]]
        sFB4 = [pFB[0], pFB[1], pSLB[0], pSLB[1]]
        sOUT4 = [pOUT[0], pOUT[1], pNT[0], pNT[1]]
        vsd_hist = {}
        P.hz = True
        for k in range(32 + 2):
            if k < 32:
                bx = k % NB
                if k >= NB:
                    P.waitt("gpsimd", t_out[k - NB])
                    P.waitt("sync", t_out[k - NB])
                t_hb[bx] = P.op("gpsimd",
                                lambda e, b=bx, k=k: e.indirect_dma_start(
                                    out=hband[b][:], out_offset=None,
                                    in_=ag_h_out.ap(),
                                    in_offset=IndirectOffsetOnAxis(
                                        spos_own[:, k:k + 1], 0),
                                    bounds_check=bregT, oob_is_err=False),
                                sHB4[bx], 16)
                t_fb[bx] = P.dma("sync", sFB4[bx], fband[bx][:],
                                 feat_d[k * 128:(k + 1) * 128, :])
            if k < 2:
                continue
            kk_ = k - 2

            def seg1(kk_=kk_):
                bx = kk_ % NB
                px = kk_ % 2
                P.wait("vector", sHB4[bx], t_hb[bx])
                P.waitt("vector", t_fb[bx])
                P.op("vector", lambda e, b=bx, p=px: e.tensor_copy(
                    hf2[p][:], hband[b][:]))
                P.op("vector", lambda e, b=bx, p=px: e.scalar_tensor_tensor(
                    out=vband[p][:], in0=fband[b][:], scalar=2.0, in1=hf2[p][:],
                    op0=AOT.mult, op1=AOT.add))
                P.op("vector", lambda e, p=px: e.reduce_sum(lsum[p][:],
                                                            vband[p][:], axis=AXX))
                P.op("vector", lambda e, p=px: e.tensor_tensor(
                    sq2[p][:], vband[p][:], vband[p][:], AOT.mult))
                P.op("vector", lambda e, p=px: e.reduce_sum(lsq[p][:], sq2[p][:],
                                                            axis=AXX))
                P.op("vector", lambda e, p=px: e.tensor_scalar_mul(
                    lmu[p][:], lsum[p][:], 1.0 / D))
                P.op("vector", lambda e, p=px: e.tensor_scalar_mul(
                    lvar[p][:], lsq[p][:], 1.0 / D))
                P.op("vector", lambda e, p=px: e.tensor_tensor(
                    sq2[p][:, 0:1], lmu[p][:], lmu[p][:], AOT.mult))
                vv = P.op("vector", lambda e, p=px: e.tensor_tensor(
                    lvar[p][:], lvar[p][:], sq2[p][:, 0:1], AOT.subtract), sV)
                P.wait("scalar", sV, vv)
                return P.op("scalar", lambda e, p=px: e.activation(
                    out=lsd[p][:], in_=lvar[p][:], func=AF.Sqrt,
                    bias=seps[:]), sA)

            def seg2(kk_, vsd):
                bx = kk_ % NB
                px = kk_ % 2
                P.wait("vector", sA, vsd)
                P.op("vector", lambda e, p=px: e.reciprocal(lrstd[p][:],
                                                            lsd[p][:]))
                vo = P.op("vector", lambda e, b=bx, p=px: e.tensor_scalar(
                    out=oband[b][:], in0=vband[p][:], scalar1=lmu[p][:],
                    scalar2=lrstd[p][:], op0=AOT.subtract, op1=AOT.mult), sV)
                P.wait("sync", sV, vo)
                t_out.append(P.dma("sync", sOUT4[bx],
                                   out_t[kk_ * 128:(kk_ + 1) * 128, :],
                                   oband[bx][:]))

            vsd_k = seg1()
            if kk_ >= 1:
                seg2(kk_ - 1, vsd_hist[kk_ - 1])
            vsd_hist[kk_] = vsd_k
            if kk_ == 31:
                seg2(31, vsd_k)
        P.hz = False
        stI.close()
    except _Stop:
        for _st in reversed(list(_open_stacks)):
            _st.close()

    if debug:
        for s in all_dma_sems:
            P.settle("gpsimd", s)
        if phases >= 3:
            P.dma("gpsimd", cDBG, dbg["d_y"], y_d)
        if phases >= 4:
            P.dma("gpsimd", cDBG, dbg["d_feat"], feat_d)
            P.dma("gpsimd", cDBG, dbg["d_norm"], norm_d)
        if phases >= 5:
            P.dma("gpsimd", cDBG, dbg["d_scores"], ag_sc_in.ap())
        if phases >= 6:
            P.dma("gpsimd", cDBG, dbg["d_agsc"], ag_sc_out.ap())
        if phases >= 8:
            P.dma("gpsimd", cDBG, dbg["d_slab"], slab_d)
            P.dma("gpsimd", cDBG, dbg["d_h"], ag_h_in.ap())

    for s in all_dma_sems:
        P.settle("gpsimd", s)
    P.wait("gpsimd", cc, cc.n)
    P.check_deadlock()

    with nc.Block() as block:
        @block.sync
        def _(e):
            P.replay("sync", e)

        @block.scalar
        def _(e):
            P.replay("scalar", e)

        @block.tensor
        def _(e):
            _g = PEGuard(e, sMM.h)
            P.replay("tensor", _g)
            assert _g._n == P.npe, (_g._n, P.npe)

        @block.vector
        def _(e):
            P.replay("vector", e)

        @block.gpsimd
        def _(e):
            P.replay("gpsimd", e)

    stk.close()
    return nc


def host_inputs(inputs):
    x = np.ascontiguousarray(inputs["L0_attn_x"])
    wq = inputs["L0_attn_w_q"]; wk = inputs["L0_attn_w_k"]
    wv = inputs["L0_attn_w_v"]; wo = inputs["L0_attn_w_o"]
    cent = inputs["L0_ffn_centroids"]
    g = inputs["L0_ffn_ln_g"]; bb = inputs["L0_ffn_ln_b"]
    w1 = inputs["L0_ffn_w1"]; b1 = inputs["L0_ffn_b1"]
    w2 = inputs["L0_ffn_w2"]; b2 = inputs["L0_ffn_b2"]

    inv_freq = (10000.0 ** (-(np.arange(HALF, dtype=np.float64) * 2.0 / DH)))
    t = np.arange(S, dtype=np.float64)
    ang = t[:, None] * inv_freq[None, :]
    cos_t = np.cos(ang).astype(np.float32)
    sin_t = np.sin(ang).astype(np.float32)

    ident = np.eye(128, dtype=np.float32)
    utfull = (np.arange(128)[:, None] < np.arange(128)[None, :]).astype(np.float32)
    utown = (utfull * (np.arange(128)[:, None] // 16 ==
                       np.arange(128)[None, :] // 16)).astype(np.float32)
    oind = (np.arange(128)[:, None] // 16 == np.arange(8)[None, :]).astype(np.float32)
    o8t = np.ascontiguousarray(oind.T)
    u8 = (np.arange(8)[:, None] < np.arange(8)[None, :]).astype(np.float32)
    onesf = np.ones((128, 128), np.float32)
    m0 = np.zeros((2, 128), np.float32); m0[0, :] = 1.0
    m1 = np.zeros((2, 128), np.float32); m1[1, :] = 1.0
    iotaj = (np.arange(128)[:, None] + 128 * np.arange(32)[None, :]).astype(np.float32)
    masks = np.zeros((128, 4 * 512), np.float32)
    for k in range(4):
        masks[:, k * 512:(k + 1) * 512] = (
            (128 * k + np.arange(128))[:, None] <= np.arange(512)[None, :])
    e4096 = np.tile((np.arange(8) * 4096).astype(np.float32)[None, :], (128, 1))

    common = dict(
        wqT=np.ascontiguousarray(wq.T), wkT=np.ascontiguousarray(wk.T),
        wvT=np.ascontiguousarray(wv.T), woT=np.ascontiguousarray(wo.T),
        centT=np.ascontiguousarray(cent.T),
        g_rep=np.ascontiguousarray(np.broadcast_to(np.asarray(g, np.float32)[None, :],
                                                   (128, D))),
        b_rep=np.ascontiguousarray(np.broadcast_to(np.asarray(bb, np.float32)[None, :],
                                                   (128, D))),
        cos_in=cos_t, sin_in=sin_t, ident=ident, utfull=utfull, utown=utown,
        oind=oind, o8t=o8t, u8=u8, onesf=onesf, m0=m0, m1=m1, iotaj=iotaj,
        masks=masks, e4096=e4096,
    )
    in_maps = []
    for c in range(8):
        im = dict(common)
        im["x"] = np.ascontiguousarray(x[c])
        im["w1e"] = np.ascontiguousarray(w1[c])
        im["w2e"] = np.ascontiguousarray(w2[c])
        im["b1t"] = np.ascontiguousarray(np.asarray(b1[c]).reshape(FF // 128, 128).T)
        im["b2t"] = np.ascontiguousarray(np.asarray(b2[c]).reshape(D // 128, 128).T)
        im["own32"] = (32 * c + np.arange(32, dtype=np.int32))[:, None].copy()
        im["rowsn"] = np.full((8, 1), c, np.int32)
        im["rowsb2"] = np.full((8, 1), 8 + c, np.int32)
        im["rowsnb"] = np.full((8, 1), 16 + c, np.int32)
        in_maps.append(im)
    return in_maps


def kernel(**inputs):
    in_maps = host_inputs(inputs)
    nc = build_program(debug=_DEBUG)
    res = run_bass_kernel_spmd(nc, in_maps, core_ids=list(range(8)))
    if _DEBUG:
        kernel._last = res
    out = np.stack([np.asarray(r["out"]) for r in res.results], axis=0)
    return out.astype(np.float32)


# version 21 (final)
# speedup vs baseline: 1.0264x; 1.0264x over previous
import numpy as np
import concourse.bass as bass
import concourse.mybir as mybir
from concourse.bass_core_rust import IndirectOffsetOnAxis
from concourse.bass_utils import run_bass_kernel_spmd
from contextlib import ExitStack

F32 = mybir.dt.float32
BF16 = mybir.dt.bfloat16
I32 = mybir.dt.int32
AOT = mybir.AluOpType
AF = mybir.ActivationFunctionType
AXX = mybir.AxisListType.X

B, S, D = 8, 4096, 1024
H, DH, HALF = 16, 64, 32
E, CAP = 8, 4096
T = B * S
FF = 4096
NITER = 30
BIGNEG = 1.0e6
EPS = 1e-5

_DEBUG = False


class Sem:
    def __init__(self, stk, nc, name):
        self.h = stk.enter_context(nc.semaphore(name))
        self.n = 0

    def bump(self, k=1):
        self.n += k
        return self.n


class _MMIns:

    def __init__(self, eng, semh, n):
        self._e = eng
        self._s = semh
        self._n = n

    def then_inc(self, sem, k=1):
        self._e.wait_ge(self._s, self._n)
        return self._e.nop().then_inc(sem, k)


class PEGuard:

    def __init__(self, eng, semh):
        object.__setattr__(self, "_e", eng)
        object.__setattr__(self, "_s", semh)
        object.__setattr__(self, "_n", 0)

    def __getattr__(self, a):
        return getattr(self._e, a)

    def _guard(self, ins):
        object.__setattr__(self, "_n", self._n + 1)
        ins.then_inc(self._s, 1)
        return _MMIns(self._e, self._s, self._n)

    def mm_barrier(self):
        if self._n > 0:
            self._e.wait_ge(self._s, self._n)

    def matmul(self, *a, **k):
        if self._n >= 2:
            self._e.wait_ge(self._s, self._n - 1)
        return self._guard(self._e.matmul(*a, **k))

    def transpose(self, *a, **k):
        if self._n >= 2:
            self._e.wait_ge(self._s, self._n - 1)
        return self._guard(self._e.transpose(*a, **k))


class Prog:
    def __init__(self):
        self.sched = []
        self.trace = []
        self.hz = False
        self.npe = 0
        self._jV = None
        self._jS = None

    def em(self, eng, fn):
        self.sched.append((eng, fn))

    def wait(self, eng, sem, val):
        if val is None:
            return
        self.trace.append((eng, "wait", sem.h.name, val))
        self.em(eng, lambda e, s=sem.h, v=val: e.wait_ge(s, v))

    def waitt(self, eng, tok):
        if tok is None:
            return
        self.wait(eng, tok[0], tok[1])

    def settle(self, eng, sem):
        if sem.n:
            self.wait(eng, sem, sem.n)

    def dma(self, eng, sem, out, in_, **kw):
        v = sem.bump(16)
        self.trace.append((eng, "inc", sem.h.name, 16))
        def f(e, out=out, in_=in_, kw=kw, s=sem.h):
            e.dma_start(out=out, in_=in_, **kw).then_inc(s, 16)
        self.em(eng, f)
        return (sem, v)

    def pe(self, builder, sem=None):
        self.op("tensor", builder, sem)
        self.npe += 1
        self.trace.append(("tensor", "inc", "sMM", 1))
        return self.npe

    def op(self, eng, builder, sem=None, k=1):
        if sem is None:
            self.em(eng, lambda e, b=builder: b(e))
            v = None
        else:
            v = sem.bump(k)
            self.trace.append((eng, "inc", sem.h.name, k))
            self.em(eng, lambda e, b=builder, s=sem.h, kk=k: b(e).then_inc(s, kk))
        if self.hz and eng == "vector" and self._jV is not None:
            j = self._jV
            self.em(eng, lambda e, j=j: e.tensor_copy(j[:], j[:]))
        if self.hz and eng == "scalar" and self._jS is not None:
            j = self._jS
            self.em(eng, lambda e, j=j: e.activation(out=j[:], in_=j[:],
                                                     func=AF.Copy))
        return v

    def replay(self, eng_name, eng):
        for name, fn in self.sched:
            if name == eng_name:
                fn(eng)

    def check_deadlock(self):
        from collections import defaultdict
        streams = defaultdict(list)
        for ev in self.trace:
            streams[ev[0]].append(ev)
        pc = {k: 0 for k in streams}
        sems = defaultdict(int)
        progress = True
        while progress:
            progress = False
            for eng, st in streams.items():
                while pc[eng] < len(st):
                    _, kind, s, v = st[pc[eng]]
                    if kind == "inc":
                        sems[s] += v
                        pc[eng] += 1
                        progress = True
                    elif sems[s] >= v:
                        pc[eng] += 1
                        progress = True
                    else:
                        break
        stuck = {k: (pc[k], len(v)) for k, v in streams.items() if pc[k] < len(v)}
        if stuck:
            detail = {e: streams[e][pc[e]] for e in stuck}
            raise RuntimeError(f"DEADLOCK: {stuck} {detail}")


class _Stop(Exception):
    pass


def build_program(debug=False, phases=9, race=False):
    nc = bass.Bass("TRN2", target_bir_lowering=False,
                   detect_race_conditions=race)
    stk = ExitStack()

    def dinp(name, shape, dt):
        return nc.dram_tensor(name, list(shape), dt, kind="ExternalInput").ap()

    x_in = dinp("x", [S, D], F32)
    wqT = dinp("wqT", [D, D], F32)
    wkT = dinp("wkT", [D, D], F32)
    wvT = dinp("wvT", [D, D], F32)
    woT = dinp("woT", [D, D], F32)
    centT = dinp("centT", [D, E], F32)
    w1e = dinp("w1e", [D, FF], F32)
    w2e = dinp("w2e", [FF, D], F32)
    b1t = dinp("b1t", [128, FF // 128], F32)
    b2t = dinp("b2t", [128, D // 128], F32)
    g_in = dinp("g_rep", [128, D], F32)
    b_in = dinp("b_rep", [128, D], F32)
    cos_in = dinp("cos_in", [S, HALF], F32)
    sin_in = dinp("sin_in", [S, HALF], F32)
    ident_in = dinp("ident", [128, 128], F32)
    utfull_in = dinp("utfull", [128, 128], F32)
    utown_in = dinp("utown", [128, 128], F32)
    oind_in = dinp("oind", [128, 8], F32)
    o8t_in = dinp("o8t", [8, 128], F32)
    u8_in = dinp("u8", [8, 8], F32)
    onesf_in = dinp("onesf", [128, 128], F32)
    m0_in = dinp("m0", [2, 128], F32)
    m1_in = dinp("m1", [2, 128], F32)
    iota_in = dinp("iotaj", [128, 32], F32)
    masks_in = dinp("masks", [128, 4 * 512], F32)
    own32_in = dinp("own32", [32, 1], I32)
    rowsn_in = dinp("rowsn", [8, 1], I32)
    rowsb2_in = dinp("rowsb2", [8, 1], I32)
    rowsnb_in = dinp("rowsnb", [8, 1], I32)
    e4096_in = dinp("e4096", [128, 8], F32)

    out_t = nc.dram_tensor("out", [S, D], F32, kind="ExternalOutput").ap()
    dbg = {}
    if debug:
        for nm, shape in [("d_feat", [S, D]), ("d_pos", [128, 256]),
                          ("d_soffs", [128, 256]), ("d_ntab", [8, 8]),
                          ("d_idx", [128, 32]), ("d_scores", [E, S]),
                          ("d_y", [S, D]), ("d_norm", [S, D]),
                          ("d_slab", [CAP, D]), ("d_h", [CAP, D]),
                          ("d_agsc", [E * B, S])]:
            dbg[nm] = nc.dram_tensor(nm, shape, F32, kind="ExternalOutput").ap()

    qtm_d = nc.dram_tensor("qtm_d", [S, D], BF16, kind="Internal").ap()
    vtm_d = nc.dram_tensor("vtm_d", [S, D], BF16, kind="Internal").ap()
    y_d = nc.dram_tensor("y_d", [S, D], BF16, kind="Internal").ap()
    feat_d = nc.dram_tensor("feat_d", [S, D], F32, kind="Internal").ap()
    norm_d = nc.dram_tensor("norm_d", [S, D], BF16, kind="Internal").ap()
    slab_d = nc.dram_tensor("slab_d", [CAP, D], BF16, kind="Internal").ap()
    slabT_d = nc.dram_tensor("slabT_d", [128, 8 * CAP], BF16, kind="Internal").ap()
    pos_d = nc.dram_tensor("pos_d", [256, 128], F32, kind="Internal").ap()
    soffs_d = nc.dram_tensor("soffs_d", [256, 128], F32, kind="Internal").ap()
    tabs_d = nc.dram_tensor("tabs_d", [24, 128], F32, kind="Internal").ap()
    ag_sc_in = nc.dram_tensor("ag_sc_in", [E, S], F32, kind="Internal")
    ag_sc_out = nc.dram_tensor("ag_sc_out", [E * B, S], F32, kind="Internal")
    ag_nm_in = nc.dram_tensor("ag_nm_in", [S, D], BF16, kind="Internal")
    ag_nm_out = nc.dram_tensor("ag_nm_out", [T, D], BF16, kind="Internal")
    ag_h_in = nc.dram_tensor("ag_h_in", [CAP, D], BF16, kind="Internal")
    ag_h_out = nc.dram_tensor("ag_h_out", [T, D], BF16, kind="Internal")

    P = Prog()
    all_dma_sems = []

    def DS(name):
        s = Sem(stk, nc, name)
        all_dma_sems.append(s)
        return s

    def DS2(name):
        return [DS(name + "a"), DS(name + "b")]

    cc = Sem(stk, nc, "cc")
    sP = Sem(stk, nc, "sP")
    sA = Sem(stk, nc, "sA")
    sV = Sem(stk, nc, "sV")
    sMM = Sem(stk, nc, "sMM")

    cCONST = DS("cCONST")
    cCS = DS("cCS")
    cWPR = DS("cWPR")
    pX = DS2("pX")
    pXB = DS2("pXB")
    cXT = DS("cXT")
    pST = DS2("pST")
    cKT = DS("cKT")
    cVA = DS("cVA")
    cWO = DS("cWO")
    pQT = DS2("pQT")
    pYT = DS2("pYT")
    pX2 = DS2("pX2")
    pYTM = DS2("pYTM")
    pFN = DS2("pFN")
    cCENT = DS("cCENT")
    cNTD = DS("cNTD")
    cSCT = DS("cSCT")
    cSCL = DS("cSCL")
    cTAB = DS("cTAB")
    cMYT = DS("cMYT")
    cPOS = DS("cPOS")
    cOWNG = DS("cOWNG")
    pNSC = DS2("pNSC")
    pSCA = DS2("pSCA")
    cW12 = DS("cW12")
    pSGB = DS2("pSGB")
    pSLB = DS2("pSLB")
    pNT = DS2("pNT")
    pH2T = DS2("pH2T")
    cHRT = DS("cHRT")
    cHRS = DS("cHRS")
    pHB = DS2("pHB")
    pFB = DS2("pFB")
    pOUT = DS2("pOUT")
    cDBG = DS("cDBG")

    def sbt(st, name, shape, dt=F32):
        return st.enter_context(nc.sbuf_tensor(name, shape, dt))

    def pst(st, name, shape, dt=F32):
        return st.enter_context(nc.psum_tensor(name, shape, dt))

    _open_stacks = []

    def mkstack():
        st = ExitStack()
        _open_stacks.append(st)
        real_close = st.close
        def close():
            _open_stacks.remove(st)
            real_close()
        st.close = close
        return st

    ident = sbt(stk, "ident_s", [128, 128])
    utfull = sbt(stk, "utfull_s", [128, 128])
    utown = sbt(stk, "utown_s", [128, 128])
    oind = sbt(stk, "oind_s", [128, 8])
    o8t = sbt(stk, "o8t_s", [8, 128])
    u8 = sbt(stk, "u8_s", [8, 8])
    onesf = sbt(stk, "onesf_s", [128, 128])
    m0 = sbt(stk, "m0_s", [2, 128])
    m1 = sbt(stk, "m1_s", [2, 128])
    iotaj = sbt(stk, "iotaj_s", [128, 32])
    masksb = sbt(stk, "masksb", [128, 4 * 512], BF16)
    onescol = sbt(stk, "onescol", [128, 1], BF16)
    onescolf = sbt(stk, "onescolf", [128, 1])
    zeros65 = sbt(stk, "zeros65", [128, 65], BF16)
    ones1x64 = sbt(stk, "ones1x64", [1, 64])
    b1s = sbt(stk, "b1s", [128, FF // 128])
    b2s = sbt(stk, "b2s", [128, D // 128])
    own32 = sbt(stk, "own32_s", [32, 1], I32)
    rowsn = sbt(stk, "rowsn_s", [8, 1], I32)
    rowsb2 = sbt(stk, "rowsb2_s", [8, 1], I32)
    rowsnb = sbt(stk, "rowsnb_s", [8, 1], I32)
    e4096 = sbt(stk, "e4096_s", [128, 8])
    seps = sbt(stk, "seps", [128, 1])
    seps2 = sbt(stk, "seps2", [128, 1])
    junkV = sbt(stk, "junkV", [128, 1])
    junkS = sbt(stk, "junkS", [128, 1])
    idx_i = sbt(stk, "idx_i", [128, 32], I32)
    sidx_own = sbt(stk, "sidx_own", [128, 32], I32)
    spos_own = sbt(stk, "spos_own", [128, 32], I32)

    stCst = ExitStack()
    masksf = sbt(stCst, "masksf", [128, 4 * 512])
    for ap_in, ap_sb in [(ident_in, ident), (utfull_in, utfull), (utown_in, utown),
                         (oind_in, oind), (o8t_in, o8t), (u8_in, u8),
                         (onesf_in, onesf), (m0_in, m0), (m1_in, m1),
                         (iota_in, iotaj), (masks_in, masksf),
                         (b1t, b1s), (b2t, b2s), (own32_in, own32),
                         (rowsn_in, rowsn), (rowsb2_in, rowsb2),
                         (rowsnb_in, rowsnb), (e4096_in, e4096)]:
        P.dma("sync", cCONST, ap_sb[:], ap_in)
    bregS = nc.gpsimd.alloc_register("bndS")
    bregT = nc.gpsimd.alloc_register("bndT")
    P.op("gpsimd", lambda e: e.reg_mov(bregS, S - 1))
    P.op("gpsimd", lambda e: e.reg_mov(bregT, T - 1))
    P.settle("vector", cCONST)
    P.op("vector", lambda e: e.tensor_copy(masksb[:], masksf[:]))
    P.op("vector", lambda e: e.memset(onescol[:], 1.0))
    P.op("vector", lambda e: e.memset(onescolf[:], 1.0))
    P.op("vector", lambda e: e.memset(zeros65[:], 0.0))
    P.op("vector", lambda e: e.memset(ones1x64[:], 1.0))
    P.op("vector", lambda e: e.memset(seps2[:], EPS * EPS))
    v_consts = P.op("vector", lambda e: e.memset(seps[:], EPS), sV)
    P.op("vector", lambda e: e.memset(junkV[:], 0.0))
    P.op("vector", lambda e: e.memset(junkS[:], 0.0))
    P._jV, P._jS = junkV, junkS

    import os as _os
    _FM = int(_os.environ.get("FMODE", "3"))

    def phase_fence():
        if _FM & 1:
            vV = P.op("vector", lambda e: e.tensor_copy(junkV[:], junkV[:]), sV)
            vS = P.op("scalar", lambda e: e.activation(out=junkS[:], in_=junkS[:],
                                                       func=AF.Copy), sA)
            vT = P.op("tensor", lambda e: e.mm_barrier() or e.nop(), sP)
        for eng in ("sync", "scalar", "vector", "tensor", "gpsimd"):
            if _FM & 2:
                for s_ in all_dma_sems:
                    P.settle(eng, s_)
            if _FM & 1:
                P.wait(eng, sV, vV)
                P.wait(eng, sA, vS)
                P.wait(eng, sP, vT)
    stCst.close()

    try:
        stKT = mkstack()
        kT = sbt(stKT, "kT", [128, 8 * S], BF16)
        stB = mkstack()
        cosb = sbt(stB, "cosb", [128, 32 * HALF])
        sinb = sbt(stB, "sinb", [128, 32 * HALF])
        P.dma("sync", cCS, cosb[:], cos_in.rearrange("(c p) m -> p c m", p=128))
        P.dma("sync", cCS, sinb[:], sin_in.rearrange("(c p) m -> p c m", p=128))
        wq_sb = sbt(stB, "wq_sb", [128, 8 * D], BF16)
        wk_sb = sbt(stB, "wk_sb", [128, 8 * D], BF16)
        wv_sb = sbt(stB, "wv_sb", [128, 8 * D], BF16)
        for w_in, w_sb in [(wqT, wq_sb), (wkT, wk_sb), (wvT, wv_sb)]:
            P.dma("gpsimd", cWPR, w_sb[:], w_in.rearrange("(c p) i -> p c i", p=128))

        xin = [sbt(stB, f"xin{i}", [128, D]) for i in range(4)]
        xbc = [sbt(stB, f"xbc{i}", [128, D], BF16) for i in range(4)]
        xTc = [sbt(stB, f"xTc{i}", [128, 8 * 128], BF16) for i in range(2)]
        identb = sbt(stB, "identb", [128, 128], BF16)
        P.wait("vector", sV, v_consts)
        v_identb = P.op("vector", lambda e: e.tensor_copy(identb[:], ident[:]), sV)

        stQ = mkstack()
        pq = pst(stQ, "pq", [128, 1024])
        pk = pst(stQ, "pk", [128, 1024])
        pv = pst(stQ, "pv", [128, 1024])
        ptx = [pst(stQ, f"ptx{i}", [128, 128], BF16) for i in range(2)]
        qf = sbt(stB, "qf", [128, D])
        kf = sbt(stB, "kf", [128, D])
        qr = [sbt(stB, f"qr{i}", [128, D], BF16) for i in range(2)]
        kr = [sbt(stB, f"kr{i}", [128, D], BF16) for i in range(2)]
        vtm = [sbt(stB, f"vtm{i}", [128, D], BF16) for i in range(2)]
        t1 = sbt(stB, "t1", [128, HALF * H])
        t2 = sbt(stB, "t2", [128, HALF * H])
        t1v = t1[:].rearrange("p (h m one) -> p h m one", h=H, one=1)
        t2v = t2[:].rearrange("p (h m one) -> p h m one", h=H, one=1)

        P.settle("tensor", cWPR)
        P.settle("vector", cCS)
        P.wait("tensor", sV, v_identb)

        def rope_views(buf):
            v4 = buf[:].rearrange("p (h m t) -> p h m t", h=H, m=HALF, t=2)
            return v4[:, :, :, 0:1], v4[:, :, :, 1:2]

        v_evac = None
        v_rope = None
        pend_ktr = []
        pX4 = [pX[0], pX[1], pXB[0], pXB[1]]
        v_xtc = [None, None, None, None]
        v_ptx_ev = [None, None]
        v_kr_free = [None, None]
        t_qst = [None, None]
        t_vst = [None, None]
        t_vst_hist = []
        for tc in range(32):
            bx = tc % 2
            b4 = tc % 4
            P.waitt("sync", t_vst_hist[tc - 4] if tc >= 4 else None)
            t_x = P.dma("sync", pX4[b4], xin[b4][:],
                        x_in[tc * 128:(tc + 1) * 128, :])
            P.waitt("scalar", t_x)
            P.wait("scalar", sMM, v_xtc[b4])
            vv = P.op("scalar", lambda e, b=b4: e.activation(
                out=xbc[b][:], in_=xin[b][:], func=AF.Copy), sA)
            P.wait("tensor", sA, vv)
            for dc in range(8):
                pb = dc % 2
                P.wait("tensor", sV, v_ptx_ev[pb])
                ntr = P.pe(
                     lambda e, b=b4, dc=dc, pb=pb: e.transpose(
                         out=ptx[pb][:], in_=xbc[b][:, dc * 128:(dc + 1) * 128],
                         identity=identb[:]))
                P.wait("vector", sMM, ntr)
                v_ptx_ev[pb] = P.op(
                    "vector",
                    lambda e, b=bx, dc=dc, pb=pb: e.tensor_copy(
                        xTc[b][:, dc * 128:(dc + 1) * 128], ptx[pb][:]), sV)
            v_xtc[b4] = ntr
            P.wait("tensor", sV, v_ptx_ev[1])
            P.wait("tensor", sA, v_evac)
            for dc in range(8):
                lhs = xTc[bx][:, dc * 128:(dc + 1) * 128]
                for (w_sb, pp) in [(wq_sb, pq), (wk_sb, pk), (wv_sb, pv)]:
                    for ih in range(2):
                        rhs = w_sb[:, dc * D + ih * 512: dc * D + ih * 512 + 512]
                        P.pe(
                             lambda e, o=pp[:, ih * 512:ih * 512 + 512], l=lhs, r=rhs,
                                    s0=(dc == 0), s1=(dc == 7):
                             e.matmul(out=o, lhsT=l, rhs=r, start=s0, stop=s1))
            v_mm = P.npe
            while pend_ktr:
                pend_ktr.pop(0)()
            P.wait("scalar", sMM, v_mm)
            P.wait("scalar", sV, v_rope)
            P.waitt("scalar", t_vst[bx])
            P.op("scalar", lambda e: e.activation(out=qf[:], in_=pq[:], func=AF.Copy))
            P.op("scalar", lambda e: e.activation(out=kf[:], in_=pk[:], func=AF.Copy))
            v_evac = P.op("scalar", lambda e, b=bx: e.activation(
                out=vtm[b][:], in_=pv[:], func=AF.Copy), sA)
            P.wait("vector", sA, v_evac)
            P.waitt("vector", t_qst[bx])
            P.wait("vector", sMM, v_kr_free[bx])
            cosv = cosb[:, tc * HALF:(tc + 1) * HALF] \
                .rearrange("p (o1 m o2) -> p o1 m o2", o1=1, o2=1) \
                .to_broadcast([128, H, HALF, 1])
            sinv = sinb[:, tc * HALF:(tc + 1) * HALF] \
                .rearrange("p (o1 m o2) -> p o1 m o2", o1=1, o2=1) \
                .to_broadcast([128, H, HALF, 1])
            for (src, dst) in [(qf, qr[bx]), (kf, kr[bx])]:
                se, so = rope_views(src)
                de, do = rope_views(dst)
                P.op("vector", lambda e, a=se, c=cosv: e.tensor_tensor(t1v, a, c,
                                                                       AOT.mult))
                P.op("vector", lambda e, a=so, s=sinv: e.tensor_tensor(t2v, a, s,
                                                                       AOT.mult))
                P.op("vector", lambda e, d=de: e.tensor_tensor(d, t1v, t2v,
                                                               AOT.subtract))
                P.op("vector", lambda e, a=se, s=sinv: e.tensor_tensor(t1v, a, s,
                                                                       AOT.mult))
                P.op("vector", lambda e, a=so, c=cosv: e.tensor_tensor(t2v, a, c,
                                                                       AOT.mult))
                v_rope = P.op("vector", lambda e, d=do: e.tensor_tensor(d, t1v, t2v,
                                                                        AOT.add), sV)
            def emit_ktr(tc=tc, bx=bx, v_rope=v_rope):
                P.wait("tensor", sV, v_rope)
                for dc in range(8):
                    pb = dc % 2
                    P.wait("tensor", sV, v_ptx_ev[pb])
                    nktr = P.pe(
                         lambda e, b=bx, dc=dc, pb=pb: e.transpose(
                             out=ptx[pb][:],
                             in_=kr[b][:, dc * 128:(dc + 1) * 128],
                             identity=identb[:]))
                    P.wait("vector", sMM, nktr)
                    v_ptx_ev[pb] = P.op(
                        "vector",
                        lambda e, dc=dc, tc=tc, pb=pb: e.tensor_copy(
                            kT[:, dc * S + tc * 128: dc * S + (tc + 1) * 128],
                            ptx[pb][:]), sV)
                v_kr_free[bx] = nktr
            pend_ktr.append(emit_ktr)
            P.wait("gpsimd", sV, v_rope)
            bst = pST[bx]
            t_qst[bx] = P.dma("gpsimd", bst, qtm_d[tc * 128:(tc + 1) * 128, :],
                              qr[bx][:])
            t_vst[bx] = P.dma("gpsimd", bst, vtm_d[tc * 128:(tc + 1) * 128, :],
                              vtm[bx][:])
            t_vst_hist.append(t_vst[bx])
        while pend_ktr:
            pend_ktr.pop(0)()
        stQ.close()
        stB.close()
        if phases <= 2:
            raise _Stop()

        phase_fence()
        stC = mkstack()
        vaug2 = [sbt(stC, f"vaug2{i}", [128, 32 * 2 * 65], BF16) for i in range(2)]
        wo_sb = sbt(stC, "wo_sb", [128, 8 * D], BF16)
        P.dma("gpsimd", cWO, wo_sb[:], woT.rearrange("(c p) i -> p c i", p=128))
        P.settle("sync", pST[0])
        P.settle("sync", pST[1])
        P.settle("scalar", pST[0])
        P.settle("scalar", pST[1])
        v_ones = None
        for b in range(2):
            vc_ = vaug2[b][:].rearrange("p (j h c) -> p (j h) c", j=32, h=2, c=65)
            v_ones = P.op("vector", lambda e, v=vc_: e.memset(v[:, :, 64:65], 1.0),
                          sV)
        pVA = [cHRT, cHRS]

        def load_vaug(hp, b):
            tok = None
            for h_ in range(2):
                srcv = vtm_d[:, hp * 128 + h_ * 64: hp * 128 + h_ * 64 + 64] \
                    .rearrange("(a p) d -> p a d", p=128)
                dstv = vaug2[b][:].rearrange("p (j h c) -> p h j c", j=32, h=2,
                                             c=65)[:, h_, :, 0:64]
                tok = P.dma("scalar", pVA[b], dstv, srcv)
            return tok

        qTit = [sbt(stC, f"qTit{i}", [128, 8 * 512], BF16) for i in range(2)]
        expb = [[sbt(stC, f"exp{hh}{i}", [128, 512], BF16) for i in range(2)]
                for hh in range(2)]
        mbuf = [[sbt(stC, f"mbuf{hh}{i}", [128, 512], BF16) for i in range(2)]
                for hh in range(2)]
        ebf = [[sbt(stC, f"ebf{hh}{i}", [128, 512]) for i in range(2)]
               for hh in range(2)]
        accE = [sbt(stC, f"accE{hh}", [128, 512]) for hh in range(2)]
        masksC = sbt(stC, "masksC", [128, 4 * 512])
        v_mkc = P.op("vector", lambda e: e.tensor_copy(masksC[:], masksb[:]), sV)
        P.wait("scalar", sV, v_mkc)
        attnT = sbt(stC, "attnT", [128, 8 * 512], BF16)
        rden = sbt(stC, "rden", [1, 1024])
        rdenb = sbt(stC, "rdenb", [64, 2 * 512])
        ytmp = [sbt(stC, f"ytmp{i}", [128, 512], BF16) for i in range(2)]
        ppS2 = [[pst(stC, f"ppS{hh}{pp}", [128, 512]) for pp in range(2)]
                for hh in range(2)]
        pAV = [pst(stC, f"pAV{hh}", [65, 512]) for hh in range(2)]
        pden = pst(stC, "pden", [64, 512])
        pmisc = pst(stC, "pmisc", [128, 512])

        P.settle("tensor", cWO)
        P.wait("tensor", sV, v_ones)
        P.wait("scalar", sV, v_ones)
        t_va = [None, None]
        t_va[0] = load_vaug(0, 0)
        t_va[1] = load_vaug(1, 1)

        for dc in range(8):
            P.dma("sync", pQT[0], qTit[0][:, dc * 512:(dc + 1) * 512],
                  qtm_d[0:512, dc * 128:(dc + 1) * 128], transpose=True)

        v_attnT_w = None
        v_oproj_mm_end = None
        t_ytst = [None, None]

        for it in range(8):
            P.wait("tensor", pQT[it % 2], 16 * 8 * (it // 2 + 1))
            qTb = qTit[it % 2]
            first_scb = [None]
            for hp in range(8):
                vb = hp % 2
                hpi = it * 8 + hp
                P.waitt("tensor", t_va[vb])
                vstat = vaug2[vb]
                if hp == 1 and it < 7:
                    P.wait("sync", sMM, first_scb[0])
                    for dc in range(8):
                        P.dma("sync", pQT[(it + 1) % 2],
                              qTit[(it + 1) % 2][:, dc * 512:(dc + 1) * 512],
                              qtm_d[(it + 1) * 512:(it + 2) * 512,
                                    dc * 128:(dc + 1) * 128], transpose=True)
                av_first = [True, True]
                if it == 0:
                    for half in (0, 1):
                        P.pe(
                             lambda e, o=pAV[half]:
                             e.matmul(out=o[0:65, :], lhsT=zeros65[:],
                                      rhs=masksb[:, 0:512], start=True, stop=False))
                        av_first[half] = False
                v_scb = {}
                v_exp = {}
                v_ebr = {}
                first_acc = [True, True]
                v_accl = [None, None]

                def sc_round(jt, hp=hp, qTb=qTb, it=it):
                    hi_t = jt >= 4 * it
                    for half in (0, 1):
                        P.wait("tensor", sA, v_exp.get((jt - 2, half)))
                        lhs = kT[64 * half:64 * half + 64,
                                 hp * S + jt * 128: hp * S + (jt + 1) * 128]
                        rhs = qTb[64 * half:64 * half + 64,
                                  hp * 512:(hp + 1) * 512]
                        nsc = P.pe(
                             lambda e, o=ppS2[jt % 2][half], l=lhs, r=rhs,
                                    tp=(64 * half, 0):
                             e.matmul(out=o[:], lhsT=l, rhs=r, start=True,
                                      stop=True, tile_position=tp))
                    v_scb[jt] = nsc
                    if first_scb[0] is None:
                        first_scb[0] = v_scb[jt]
                    for half in (0, 1):
                        P.wait("scalar", sMM, v_scb[jt])
                        if hi_t:
                            P.wait("scalar", sV, v_ebr.get((jt - 2, half)))
                        dst = ebf[half][jt % 2] if hi_t else expb[half][jt % 2]
                        v_exp[(jt, half)] = P.op(
                            "scalar",
                            lambda e, o=dst, i=ppS2[jt % 2][half]:
                            e.activation(out=o[:], in_=i[:], func=AF.Exp,
                                         scale=0.125), sA)

                def av_round(jt, hp=hp, it=it):
                    lo_t = jt <= 4 * it - 1
                    di_t = (4 * it <= jt <= 4 * it + 3)
                    for half in (0, 1):
                        eb = expb[half][jt % 2]
                        P.wait("tensor", sA, v_exp[(jt, half)])
                        stat_v1 = vstat[:, (jt * 2 + half) * 65:
                                        (jt * 2 + half) * 65 + 65]
                        stat_v = vstat[:, (jt * 2 + half) * 65:
                                       (jt * 2 + half) * 65 + 64]
                        if lo_t:
                            P.pe(
                                 lambda e, o=pAV[half], l=stat_v1, r=eb,
                                        s0=av_first[half]:
                                 e.matmul(out=o[:], lhsT=l, rhs=r[:], start=s0,
                                          stop=False))
                            av_first[half] = False
                        elif di_t:
                            fb = ebf[half][jt % 2]
                            kk = jt - 4 * it
                            P.wait("vector", sA, v_exp[(jt, half)])
                            P.wait("vector", sMM, v_scb[jt])
                            vm = P.op("vector",
                                      lambda e, o=mbuf[half][jt % 2], a=fb,
                                             m=masksC[:, kk * 512:(kk + 1) * 512]:
                                      e.tensor_tensor(o[:], a[:], m, AOT.mult), sV)
                            if first_acc[half]:
                                vacc = P.op("vector",
                                            lambda e, a=accE[half], f=fb:
                                            e.tensor_copy(a[:], f[:]), sV)
                                first_acc[half] = False
                            else:
                                vacc = P.op("vector",
                                            lambda e, a=accE[half], f=fb:
                                            e.tensor_tensor(a[:], a[:], f[:],
                                                            AOT.add), sV)
                            v_ebr[(jt, half)] = vacc
                            v_accl[half] = vacc
                            P.wait("tensor", sV, vm)
                            P.pe(
                                 lambda e, o=pAV[half], l=stat_v,
                                        r=mbuf[half][jt % 2], s0=av_first[half]:
                                 e.matmul(out=o[0:64, :], lhsT=l, rhs=r[:],
                                          start=s0, stop=False))
                            av_first[half] = False
                        else:
                            fb = ebf[half][jt % 2]
                            P.wait("vector", sA, v_exp[(jt, half)])
                            if first_acc[half]:
                                vacc = P.op("vector",
                                            lambda e, a=accE[half], f=fb:
                                            e.tensor_copy(a[:], f[:]), sV)
                                first_acc[half] = False
                            else:
                                vacc = P.op("vector",
                                            lambda e, a=accE[half], f=fb:
                                            e.tensor_tensor(a[:], a[:], f[:],
                                                            AOT.add), sV)
                            v_ebr[(jt, half)] = vacc
                            v_accl[half] = vacc

                for jt in range(32):
                    sc_round(jt)
                    if jt >= 1:
                        av_round(jt - 1)
                av_round(31)
                for half in (0, 1):
                    if not first_acc[half]:
                        P.wait("tensor", sV, v_accl[half])
                        P.pe(
                             lambda e, o=pAV[half], r=accE[half]:
                             e.matmul(out=o[64:65, :], lhsT=onescolf[:], rhs=r[:],
                                      start=False, stop=False,
                                      tile_position=(0, 64)))
                for half in (0, 1):
                    P.pe(
                         lambda e, o=pAV[half]:
                         e.matmul(out=o[0:65, :], lhsT=zeros65[:],
                                  rhs=masksb[:, 0:512], start=False, stop=True))
                v_av = P.npe
                if hpi < 62:
                    P.wait("scalar", sMM, v_av)
                    t_va[vb] = load_vaug((hpi + 2) % 8, vb)
                P.wait("vector", sMM, v_av)
                vr = [None, None]
                for half in (0, 1):
                    vr[half] = P.op(
                        "vector",
                        lambda e, o=rden[0:1, half * 512:(half + 1) * 512],
                               i=pAV[half][64:65, :]:
                        e.reciprocal(o, i), sV)
                v_den_evac = None
                for half in (0, 1):
                    P.wait("tensor", sV, vr[half])
                    P.wait("tensor", sA, v_den_evac)
                    P.pe(
                         lambda e, r=rden[0:1, half * 512:(half + 1) * 512]:
                         e.matmul(out=pden[:], lhsT=ones1x64[:], rhs=r,
                                  start=True, stop=True))
                    P.wait("scalar", sMM, P.npe)
                    v_den_evac = P.op(
                        "scalar",
                        lambda e, o=rdenb[:, half * 512:(half + 1) * 512]:
                        e.activation(out=o, in_=pden[:], func=AF.Copy), sA)
                    P.wait("vector", sA, v_den_evac)
                    v_attnT_w = P.op(
                        "vector",
                        lambda e, o=attnT[64 * half:64 * half + 64,
                                          hp * 512:(hp + 1) * 512],
                               a=pAV[half][0:64, :],
                               r=rdenb[:, half * 512:(half + 1) * 512]:
                        e.tensor_tensor(o, a, r, AOT.mult), sV)
                P.wait("tensor", sV, v_attnT_w)
            if phases <= 2.8:
                continue
            v_ytev = None
            for ts in range(4):
                for ih in range(2):
                    pb = (2 * ts + ih) % 2
                    P.wait("tensor", sA, v_ytev)
                    for jc in range(8):
                        lhs = attnT[:, jc * 512 + ts * 128: jc * 512 + (ts + 1) * 128]
                        rhs = wo_sb[:, jc * D + ih * 512: jc * D + ih * 512 + 512]
                        P.pe(
                             lambda e, l=lhs, r=rhs, s0=(jc == 0), s1=(jc == 7):
                             e.matmul(out=pmisc[:], lhsT=l, rhs=r, start=s0, stop=s1))
                    P.wait("scalar", sMM, P.npe)
                    P.waitt("scalar", t_ytst[pb])
                    v_ytev = P.op("scalar",
                                  lambda e, o=ytmp[pb]: e.activation(
                                      out=o[:], in_=pmisc[:], func=AF.Copy), sA)
                    P.wait("gpsimd", sA, v_ytev)
                    t_ytst[pb] = P.dma(
                        "gpsimd", pYT[pb],
                        y_d[it * 512 + ts * 128: it * 512 + (ts + 1) * 128,
                            ih * 512:(ih + 1) * 512],
                        ytmp[pb][:])
            v_oproj_mm_end = P.op("tensor", lambda e: e.mm_barrier() or e.nop(), sP)
        stC.close()
        stKT.close()
        if phases <= 3:
            raise _Stop()

        phase_fence()
        stNT = mkstack()
        nTfull = sbt(stNT, "nTfull", [128, 8 * S], BF16)
        nTlo = sbt(stNT, "nTlo", [128, 8 * S], BF16)
        stD = mkstack()
        g_rep = sbt(stD, "g_rep_s", [128, D])
        b_rep = sbt(stD, "b_rep_s", [128, D])
        P.settle("sync", pYT[0])
        P.settle("sync", pYT[1])
        P.settle("scalar", pYT[0])
        P.settle("scalar", pYT[1])
        P.dma("sync", cCS, g_rep[:], g_in)
        P.dma("sync", cCS, b_rep[:], b_in)
        xin2 = [sbt(stD, f"xin2{i}", [128, D]) for i in range(2)]
        ytm = [sbt(stD, f"ytm{i}", [128, D], BF16) for i in range(2)]
        ytf = sbt(stD, "ytf", [128, D])
        sbuf_s = sbt(stD, "sbuf_s", [128, D])
        sqsc = sbt(stD, "sqsc", [128, D])
        ssq = sbt(stD, "ssq", [128, 1])
        ssum = sbt(stD, "ssum", [128, 1])
        smu = sbt(stD, "smu", [128, 1])
        svar = sbt(stD, "svar", [128, 1])
        ssd = sbt(stD, "ssd", [128, 1])
        srstd = sbt(stD, "srstd", [128, 1])
        ssd2 = sbt(stD, "ssd2", [128, 1])
        srstd2 = sbt(stD, "srstd2", [128, 1])
        nrm0 = sbt(stD, "nrm0", [128, D])
        featb = [sbt(stD, f"featb{i}", [128, D]) for i in range(2)]
        normb = [sbt(stD, f"normb{i}", [128, D], BF16) for i in range(2)]
        normlo = [sbt(stD, f"normlo{i}", [128, D], BF16) for i in range(2)]
        normf = sbt(stD, "normf", [128, D])
        nhif = sbt(stD, "nhif", [128, D])
        identb2 = sbt(stD, "identb2", [128, 128], BF16)
        v_idb2 = P.op("vector", lambda e: e.tensor_copy(identb2[:], ident[:]), sV)
        ptxD = [pst(stD, f"ptxD{i}", [128, 128], BF16) for i in range(2)]
        P.wait("tensor", sV, v_idb2)
        P.settle("vector", cCS)
        t_fn = []
        v_ptxD_ev = [None, None]
        v_nTtr = [None, None]
        v_nT_done = None
        P.hz = True
        for tc in range(32):
            bx = tc % 2
            if tc >= 2:
                P.waitt("sync", t_fn[tc - 2])
                P.waitt("scalar", t_fn[tc - 2])
                P.waitt("vector", t_fn[tc - 2])
                P.wait("vector", sMM, v_nTtr[bx])
            t_x = P.dma("sync", pX2[bx], xin2[bx][:],
                        x_in[tc * 128:(tc + 1) * 128, :])
            t_y = P.dma("scalar", pYTM[bx], ytm[bx][:],
                        y_d[tc * 128:(tc + 1) * 128, :])
            P.waitt("scalar", t_y)
            vyc = P.op("scalar", lambda e, b=bx: e.activation(
                out=ytf[:], in_=ytm[b][:], func=AF.Copy), sA)
            P.waitt("vector", t_x)
            P.wait("vector", sA, vyc)
            vsb = P.op("vector", lambda e, b=bx: e.tensor_tensor(
                sbuf_s[:], xin2[b][:], ytf[:], AOT.add), sV)
            P.wait("scalar", sV, vsb)
            P.op("scalar", lambda e: e.activation(
                out=sqsc[:], in_=sbuf_s[:], func=AF.Copy, accum_out=ssum[:]))
            vsq = P.op("scalar", lambda e: e.activation(
                out=sqsc[:], in_=sbuf_s[:], func=AF.Square, accum_out=ssq[:]),
                sA)
            P.wait("vector", sA, vsq)
            P.op("vector", lambda e: e.tensor_scalar_mul(smu[:], ssum[:], 1.0 / D))
            P.op("vector", lambda e: e.tensor_scalar_mul(svar[:], ssq[:], 1.0 / D))
            P.op("vector", lambda e: e.tensor_tensor(ssd[:], smu[:], smu[:],
                                                     AOT.mult))
            vv = P.op("vector", lambda e: e.tensor_tensor(svar[:], svar[:], ssd[:],
                                                          AOT.subtract), sV)
            P.wait("scalar", sV, vv)
            P.op("scalar", lambda e: e.activation(out=ssd[:], in_=svar[:],
                                                  func=AF.Sqrt, bias=seps[:]))
            vsd = P.op("scalar", lambda e: e.activation(out=ssd2[:], in_=svar[:],
                                                        func=AF.Sqrt,
                                                        scale=1.0 + EPS,
                                                        bias=seps2[:]), sA)
            P.wait("vector", sA, vsd)
            P.op("vector", lambda e: e.reciprocal(srstd[:], ssd[:]))
            P.op("vector", lambda e: e.reciprocal(srstd2[:], ssd2[:]))
            P.op("vector", lambda e, b=bx: e.tensor_scalar(
                out=featb[b][:], in0=sbuf_s[:], scalar1=smu[:], scalar2=srstd[:],
                op0=AOT.subtract, op1=AOT.mult))
            P.op("vector", lambda e: e.tensor_scalar(
                out=nrm0[:], in0=sbuf_s[:], scalar1=smu[:], scalar2=srstd2[:],
                op0=AOT.subtract, op1=AOT.mult))
            P.op("vector", lambda e: e.tensor_tensor(sqsc[:], nrm0[:],
                                                     g_rep[:], AOT.mult))
            P.op("vector", lambda e: e.tensor_tensor(normf[:], sqsc[:],
                                                     b_rep[:], AOT.add))
            vn = P.op("vector", lambda e, b=bx: e.tensor_copy(normb[b][:],
                                                              normf[:]), sV)
            P.op("vector", lambda e, b=bx: e.tensor_copy(nhif[:], normb[b][:]))
            P.op("vector", lambda e: e.tensor_tensor(sqsc[:], normf[:], nhif[:],
                                                     AOT.subtract))
            vnl = P.op("vector", lambda e, b=bx: e.tensor_copy(normlo[b][:],
                                                               sqsc[:]), sV)
            P.wait("gpsimd", sV, vn)
            P.dma("gpsimd", pFN[bx], feat_d[tc * 128:(tc + 1) * 128, :],
                  featb[bx][:])
            t_fn.append(P.dma("gpsimd", pFN[bx], norm_d[tc * 128:(tc + 1) * 128, :],
                              normb[bx][:]))
            P.wait("tensor", sV, vnl)
            for src_b, dstT in ((normb, nTfull), (normlo, nTlo)):
                for dc in range(8):
                    pb = dc % 2
                    P.wait("tensor", sV, v_ptxD_ev[pb])
                    ntr = P.pe(
                         lambda e, sb_=src_b[bx], dc=dc, pb=pb: e.transpose(
                             out=ptxD[pb][:], in_=sb_[:, dc * 128:(dc + 1) * 128],
                             identity=identb2[:]))
                    P.wait("vector", sMM, ntr)
                    v_ptxD_ev[pb] = P.op(
                        "vector",
                        lambda e, dc=dc, tc=tc, pb=pb, dT=dstT: e.tensor_copy(
                            dT[:, dc * S + tc * 128: dc * S + (tc + 1) * 128],
                            ptxD[pb][:]), sV)
            v_nTtr[bx] = ntr
            v_nT_done = v_ptxD_ev[1]
        P.hz = False
        stD.close()
        if phases <= 4:
            raise _Stop()

        phase_fence()
        stE = mkstack()
        centf = sbt(stE, "centf", [128, 8 * E])
        centfl = sbt(stE, "centfl", [128, 8 * E])
        cent_hi = sbt(stE, "cent_hi", [128, 8 * E], BF16)
        cent_lo = sbt(stE, "cent_lo", [128, 8 * E], BF16)
        P.dma("gpsimd", cCENT, centf[:],
              centT.rearrange("(c p) e -> p c e", p=128))
        scT = sbt(stE, "scT", [8, S])
        ps_sc = pst(stE, "ps_sc", [8, S])
        P.settle("vector", cCENT)
        P.hz = True
        P.op("vector", lambda e: e.tensor_copy(cent_hi[:], centf[:]))
        P.op("vector", lambda e: e.tensor_copy(centfl[:], cent_hi[:]))
        P.op("vector", lambda e: e.tensor_tensor(centfl[:], centf[:], centfl[:],
                                                 AOT.subtract))
        v_csp = P.op("vector", lambda e: e.tensor_copy(cent_lo[:], centfl[:]), sV)
        P.hz = False
        P.wait("tensor", sV, v_csp)
        P.wait("tensor", sV, v_nT_done)
        for tt in range(8):
            passes = [(cent_hi, nTfull), (cent_hi, nTlo), (cent_lo, nTfull)]
            for dc in range(8):
                for pi, (cs, nT_) in enumerate(passes):
                    P.pe(
                         lambda e, o=ps_sc[:, tt * 512:(tt + 1) * 512],
                                l=cs[:, dc * E:(dc + 1) * E],
                                r=nT_[:, dc * S + tt * 512:
                                      dc * S + (tt + 1) * 512],
                                s0=(dc == 0 and pi == 0),
                                s1=(dc == 7 and pi == 2):
                         e.matmul(out=o, lhsT=l, rhs=r, start=s0, stop=s1))
        vmm = P.op("tensor", lambda e: e.mm_barrier() or e.nop(), sP)
        P.wait("vector", sP, vmm)
        v_scT = P.op("vector", lambda e: e.tensor_copy(scT[:], ps_sc[:]), sV)
        P.wait("gpsimd", sV, v_scT)
        P.dma("gpsimd", cSCT, ag_sc_in.ap(), scT[:])
        P.settle("gpsimd", cSCT)
        if phases <= 4.6:
            raise _Stop()
        v_ag1 = P.op("gpsimd", lambda e: e.collective_compute(
            "AllGather", AOT.bypass, replica_groups=[list(range(8))],
            ins=[ag_sc_in.ap().opt()], outs=[ag_sc_out.ap().opt()]), cc)
        stE.close()
        stNT.close()
        if phases <= 5:
            raise _Stop()

        phase_fence()
        stF = mkstack()
        P.hz = True
        sc_all = sbt(stF, "sc_all", [128, E * 256])
        avail = sbt(stF, "avail", [128, 256])
        sm = sbt(stF, "sm", [128, 256])
        sm2 = sbt(stF, "sm2", [128, 256])
        tmp = sbt(stF, "tmp", [128, 256])
        onesr = sbt(stF, "onesr", [128, 256])
        mx2 = sbt(stF, "mx2", [128, 2])
        vals = sbt(stF, "vals", [2, 1])
        lo = sbt(stF, "lo", [128, 1])
        hi0 = sbt(stF, "hi0", [128, 1])
        wid = [sbt(stF, f"wid{i}", [128, 1]) for i in range(2)]
        tau = sbt(stF, "tau", [128, 1])
        cnt = sbt(stF, "cnt", [128, 1])
        pred = sbt(stF, "pred", [128, 1])
        offs = sbt(stF, "offs", [128, 1])
        sel0 = sbt(stF, "sel0", [128, 256])
        sel = sbt(stF, "sel", [128, 256])
        scan1 = sbt(stF, "scan1", [128, 256])
        gt = sbt(stF, "gt", [128, 256])
        rr = sbt(stF, "rr", [128, 256])
        basespp = sbt(stF, "basespp", [128, 1])
        nbarpp = sbt(stF, "nbarpp", [128, 1])
        soffs_all = sbt(stF, "soffs_all", [128, 256])
        pos_all = sbt(stF, "pos_all", [128, 256])
        ntab = sbt(stF, "ntab", [8, 8])
        b2tab = sbt(stF, "b2tab", [8, 8])
        nbtab = sbt(stF, "nbtab", [8, 8])
        runb = sbt(stF, "runb", [8, 1])
        runsnap = sbt(stF, "runsnap", [8, 1])
        ncol = sbt(stF, "ncol", [8, 1])
        nbcol = sbt(stF, "nbcol", [8, 1])
        pad3 = [sbt(stF, f"pad{i}", [8, 128]) for i in range(3)]
        pF_tot = pst(stF, "pF_tot", [128, 1])
        pF_t2 = pst(stF, "pF_t2", [2, 128])
        pF_pp = pst(stF, "pF_pp", [128, 1])
        pF_n8 = pst(stF, "pF_n8", [8, 1])
        pF_88 = pst(stF, "pF_88", [8, 8])
        pF_32 = pst(stF, "pF_32", [128, 32])

        P.wait("sync", cc, v_ag1)
        for e_ in range(E):
            for c_ in range(B):
                P.dma("sync", cSCL,
                      sc_all[16 * c_:16 * c_ + 16, e_ * 256:(e_ + 1) * 256],
                      ag_sc_out.ap()[8 * c_ + e_:8 * c_ + e_ + 1, :]
                      .rearrange("one (p f) -> (one p) f", p=16))
        P.settle("vector", cSCL)
        for buf, valz in [(avail, 1.0), (onesr, 1.0), (soffs_all, 0.0),
                          (pos_all, 0.0)]:
            P.op("vector", lambda e, b=buf, v=valz: e.memset(b[:], v))
        for buf in (runb, ntab, b2tab, nbtab, pad3[0], pad3[1], pad3[2]):
            P.op("vector", lambda e, b=buf: e.memset(b[:], 0.0))
        v_initF = P.op("vector", lambda e: e.memset(mx2[:], 0.0), sV)
        P.wait("tensor", sV, v_initF)

        v_snap_used = None
        for e_ in range(E):
            se = sc_all[:, e_ * 256:(e_ + 1) * 256]
            if e_ < 7:
                P.op("vector", lambda e, a=se: e.tensor_tensor(tmp[:], a, avail[:],
                                                               AOT.mult))
                P.op("vector", lambda e: e.tensor_scalar(
                    out=sm2[:], in0=avail[:], scalar1=-1.0, scalar2=BIGNEG,
                    op0=AOT.add, op1=AOT.mult))
                P.op("vector", lambda e: e.tensor_tensor(sm[:], tmp[:], sm2[:],
                                                         AOT.add))
                P.op("vector", lambda e: e.tensor_tensor(sm2[:], sm2[:], tmp[:],
                                                         AOT.subtract))
                P.op("vector", lambda e: e.reduce_max(mx2[:, 0:1], sm[:], axis=AXX))
                vmx = P.op("vector", lambda e: e.reduce_max(mx2[:, 1:2], sm2[:],
                                                            axis=AXX), sV)
                P.wait("tensor", sV, vmx)
                P.pe( lambda e: e.transpose(out=pF_t2[:], in_=mx2[:],
                                                     identity=ident[:]))
                vt2 = P.op("tensor", lambda e: e.mm_barrier() or e.nop(), sP)
                P.wait("vector", sP, vt2)
                vv2 = P.op("vector", lambda e: e.reduce_max(vals[:], pF_t2[:],
                                                            axis=AXX), sV)
                P.wait("tensor", sV, vv2)
                P.pe( lambda e: e.matmul(out=pF_pp[:], lhsT=m0[:],
                                                  rhs=vals[:], start=True,
                                                  stop=True))
                vb0 = P.op("tensor", lambda e: e.mm_barrier() or e.nop(), sP)
                P.wait("vector", sP, vb0)
                vhi = P.op("vector", lambda e: e.tensor_scalar_add(hi0[:], pF_pp[:],
                                                                   1.0), sV)
                P.wait("tensor", sV, vhi)
                P.pe( lambda e: e.matmul(out=pF_pp[:], lhsT=m1[:],
                                                  rhs=vals[:], start=True,
                                                  stop=True))
                vb1 = P.op("tensor", lambda e: e.mm_barrier() or e.nop(), sP)
                P.wait("vector", sP, vb1)
                P.op("vector", lambda e: e.tensor_scalar(
                    out=lo[:], in0=pF_pp[:], scalar1=-1.0, scalar2=-1.0,
                    op0=AOT.mult, op1=AOT.add))
                P.op("vector", lambda e: e.tensor_tensor(wid[0][:], hi0[:], lo[:],
                                                         AOT.subtract))
                P.op("vector", lambda e: e.tensor_scalar(
                    out=tau[:], in0=wid[0][:], scalar1=0.5, scalar2=None,
                    op0=AOT.mult))
                P.op("vector", lambda e: e.tensor_tensor(tau[:], tau[:], lo[:],
                                                         AOT.add))
                for i in range(NITER):
                    vc = P.op("vector", lambda e: e.tensor_scalar(
                        out=tmp[:], in0=sm[:], scalar1=tau[:], scalar2=0.0,
                        op0=AOT.is_ge, op1=AOT.add, accum_out=cnt[:]), sV)
                    P.wait("tensor", sV, vc)
                    P.pe( lambda e: e.matmul(out=pF_tot[:], lhsT=onesf[:],
                                                      rhs=cnt[:], start=True,
                                                      stop=True))
                    vt = P.op("tensor", lambda e: e.mm_barrier() or e.nop(), sP)
                    P.wait("vector", sP, vt)
                    P.op("vector", lambda e: e.tensor_scalar(
                        out=pred[:], in0=pF_tot[:], scalar1=float(CAP),
                        scalar2=None, op0=AOT.is_ge))
                    wprev, wnext = wid[i % 2], wid[(i + 1) % 2]
                    P.op("vector", lambda e, wp=wprev, wn=wnext: e.tensor_scalar(
                        out=wn[:], in0=wp[:], scalar1=0.5, scalar2=None,
                        op0=AOT.mult))
                    P.op("vector", lambda e, wn=wnext: e.scalar_tensor_tensor(
                        out=lo[:], in0=pred[:], scalar=wn[:], in1=lo[:],
                        op0=AOT.mult, op1=AOT.add))
                    P.op("vector", lambda e, wn=wnext: e.tensor_scalar(
                        out=tau[:], in0=wn[:], scalar1=0.5, scalar2=None,
                        op0=AOT.mult))
                    P.op("vector", lambda e: e.tensor_tensor(tau[:], tau[:], lo[:],
                                                             AOT.add))
                P.op("vector", lambda e: e.tensor_scalar(
                    out=sel0[:], in0=sm[:], scalar1=lo[:], scalar2=None,
                    op0=AOT.is_ge))
            else:
                P.op("vector", lambda e: e.tensor_copy(sel0[:], avail[:]))
            P.op("vector", lambda e: e.tensor_tensor_scan(
                out=scan1[:], data0=onesr[:], data1=sel0[:], initial=0.0,
                op0=AOT.mult, op1=AOT.add))
            vcp = P.op("vector", lambda e: e.tensor_copy(cnt[:], scan1[:, 255:256]),
                       sV)
            P.wait("tensor", sV, vcp)
            P.pe( lambda e: e.matmul(out=pF_pp[:], lhsT=utfull[:],
                                              rhs=cnt[:], start=True, stop=True))
            vmm = P.op("tensor", lambda e: e.mm_barrier() or e.nop(), sP)
            P.wait("vector", sP, vmm)
            P.op("vector", lambda e: e.tensor_copy(offs[:], pF_pp[:]))
            P.op("vector", lambda e: e.tensor_tensor(gt[:], scan1[:], sel0[:],
                                                     AOT.subtract))
            P.op("vector", lambda e: e.tensor_scalar(
                out=gt[:], in0=gt[:], scalar1=offs[:], scalar2=None, op0=AOT.add))
            P.op("vector", lambda e: e.tensor_scalar(
                out=tmp[:], in0=gt[:], scalar1=float(CAP), scalar2=None,
                op0=AOT.is_lt))
            P.op("vector", lambda e: e.tensor_tensor(sel[:], sel0[:], tmp[:],
                                                     AOT.mult))
            P.op("vector", lambda e: e.tensor_tensor_scan(
                out=scan1[:], data0=onesr[:], data1=sel[:], initial=0.0,
                op0=AOT.mult, op1=AOT.add))
            vcp = P.op("vector", lambda e: e.tensor_copy(cnt[:], scan1[:, 255:256]),
                       sV)
            P.wait("tensor", sV, vcp)
            P.pe( lambda e: e.matmul(out=pF_pp[:], lhsT=utown[:],
                                              rhs=cnt[:], start=True, stop=True))
            P.pe( lambda e: e.matmul(out=pF_n8[:], lhsT=oind[:],
                                              rhs=cnt[:], start=True, stop=True))
            vmn = P.op("tensor", lambda e: e.mm_barrier() or e.nop(), sP)
            P.wait("vector", sP, vmn)
            P.op("vector", lambda e: e.tensor_copy(offs[:], pF_pp[:]))
            P.op("vector", lambda e: e.tensor_tensor(rr[:], scan1[:], sel[:],
                                                     AOT.subtract))
            P.op("vector", lambda e: e.tensor_scalar(
                out=rr[:], in0=rr[:], scalar1=offs[:], scalar2=None, op0=AOT.add))
            vnc = P.op("vector", lambda e: e.tensor_copy(ncol[:], pF_n8[:]), sV)
            P.op("vector", lambda e, e_=e_: e.tensor_copy(ntab[:, e_:e_ + 1],
                                                          ncol[:]))
            P.op("vector", lambda e, e_=e_: e.tensor_copy(b2tab[:, e_:e_ + 1],
                                                          runb[:]))
            P.wait("vector", sP, v_snap_used)
            vrs = P.op("vector", lambda e: e.tensor_copy(runsnap[:], runb[:]), sV)
            P.wait("tensor", sV, vnc)
            P.wait("tensor", sV, vrs)
            P.pe( lambda e: e.matmul(out=pF_n8[:], lhsT=u8[:], rhs=ncol[:],
                                              start=True, stop=True))
            P.pe( lambda e: e.matmul(out=pF_pp[:], lhsT=o8t[:],
                                              rhs=runsnap[:], start=True,
                                              stop=True))
            vb2 = P.op("tensor", lambda e: e.mm_barrier() or e.nop(), sP)
            v_snap_used = vb2
            P.wait("vector", sP, vb2)
            P.op("vector", lambda e: e.tensor_copy(nbcol[:], pF_n8[:]))
            P.op("vector", lambda e, e_=e_: e.tensor_copy(nbtab[:, e_:e_ + 1],
                                                          nbcol[:]))
            vbp = P.op("vector", lambda e: e.tensor_copy(basespp[:], pF_pp[:]), sV)
            P.wait("tensor", sV, vbp)
            P.pe( lambda e: e.matmul(out=pF_pp[:], lhsT=o8t[:],
                                              rhs=nbcol[:], start=True, stop=True))
            vnbb = P.op("tensor", lambda e: e.mm_barrier() or e.nop(), sP)
            P.wait("vector", sP, vnbb)
            P.op("vector", lambda e: e.tensor_copy(nbarpp[:], pF_pp[:]))
            P.op("vector", lambda e: e.tensor_scalar(
                out=tmp[:], in0=rr[:], scalar1=basespp[:], scalar2=None,
                op0=AOT.add))
            P.op("vector", lambda e: e.tensor_tensor(tmp[:], tmp[:], sel[:],
                                                     AOT.mult))
            P.op("vector", lambda e: e.tensor_tensor(soffs_all[:], soffs_all[:],
                                                     tmp[:], AOT.add))
            P.op("vector", lambda e, e_=e_: e.tensor_scalar(
                out=tmp[:], in0=rr[:], scalar1=nbarpp[:], scalar2=float(CAP * e_),
                op0=AOT.add, op1=AOT.add))
            P.op("vector", lambda e: e.tensor_tensor(tmp[:], tmp[:], sel[:],
                                                     AOT.mult))
            P.op("vector", lambda e: e.tensor_tensor(pos_all[:], pos_all[:],
                                                     tmp[:], AOT.add))
            P.op("vector", lambda e: e.tensor_tensor(avail[:], avail[:], sel[:],
                                                     AOT.subtract))
            P.op("vector", lambda e: e.tensor_tensor(runb[:], runb[:], ncol[:],
                                                     AOT.add))

        vtt = P.op("vector", lambda e: e.tensor_copy(mx2[0:8, 0:2], mx2[0:8, 0:2]),
                   sV)
        vpd = None
        for i, tab in enumerate((ntab, b2tab, nbtab)):
            P.wait("tensor", sV, vtt if i == 0 else vpd)
            P.pe( lambda e, t=tab: e.transpose(out=pF_88[:], in_=t[:],
                                                        identity=ident[0:8, 0:8]))
            vtr = P.op("tensor", lambda e: e.mm_barrier() or e.nop(), sP)
            P.wait("vector", sP, vtr)
            vpd = P.op("vector", lambda e, i=i: e.tensor_copy(pad3[i][:, 0:8],
                                                              pF_88[:]), sV)
            P.wait("gpsimd", sV, vpd)
            P.dma("gpsimd", cTAB, tabs_d[i * 8:(i + 1) * 8, :], pad3[i][:])
        vps = P.op("vector", lambda e: e.tensor_copy(tmp[:], pos_all[:]), sV)
        P.wait("gpsimd", sV, vps)
        P.dma("gpsimd", cPOS, soffs_d.rearrange("(p a) c -> p (a c)", p=128),
              soffs_all[:])
        P.dma("gpsimd", cPOS, pos_d.rearrange("(p a) c -> p (a c)", p=128),
              pos_all[:])
        if debug:
            P.dma("gpsimd", cDBG, dbg["d_pos"], pos_all[:])
            P.dma("gpsimd", cDBG, dbg["d_soffs"], soffs_all[:])
            P.dma("gpsimd", cDBG, dbg["d_ntab"], ntab[:])

        mytabs = [sbt(stF, f"mytab{i}", [8, 128]) for i in range(3)]
        myb = sbt(stF, "myb", [128, 24])
        gtab = sbt(stF, "gtab", [128, 8])
        fmask = sbt(stF, "fmask", [128, 32])
        dG = sbt(stF, "dG", [128, 32])
        acc = sbt(stF, "acc", [128, 32])
        idx_f = sbt(stF, "idx_f", [128, 32])
        own_f = sbt(stF, "own_f", [32, 128])

        P.settle("gpsimd", cTAB)
        for i, rowsel in enumerate((rowsn, rowsb2, rowsnb)):
            P.op("gpsimd",
                 lambda e, i=i, r=rowsel: e.indirect_dma_start(
                     out=mytabs[i][:], out_offset=None, in_=tabs_d,
                     in_offset=IndirectOffsetOnAxis(r[:], 0)), cMYT, 16)
        P.settle("tensor", cMYT)
        for i in range(3):
            P.pe(
                 lambda e, i=i: e.matmul(out=pF_32[:, i * 8:(i + 1) * 8],
                                         lhsT=onesf[0:1, :],
                                         rhs=mytabs[i][0:1, 0:8], start=True,
                                         stop=True))
        vbc = P.op("tensor", lambda e: e.mm_barrier() or e.nop(), sP)
        P.wait("vector", sP, vbc)
        P.op("vector", lambda e: e.tensor_copy(myb[:], pF_32[:, 0:24]))
        P.op("vector", lambda e: e.tensor_tensor(gtab[:], myb[:, 8:16],
                                                 myb[:, 16:24], AOT.subtract))
        P.op("vector", lambda e: e.tensor_tensor(gtab[:], gtab[:], e4096[:],
                                                 AOT.add))
        P.op("vector", lambda e: e.tensor_scalar(
            out=acc[:], in0=iotaj[:], scalar1=0.0, scalar2=gtab[:, 0:1],
            op0=AOT.mult, op1=AOT.add))
        for c_ in range(1, 8):
            P.op("vector", lambda e, c_=c_: e.tensor_scalar(
                out=fmask[:], in0=iotaj[:], scalar1=myb[:, 16 + c_:17 + c_],
                scalar2=None, op0=AOT.is_ge))
            P.op("vector", lambda e, c_=c_: e.tensor_tensor(
                dG[:], gtab[:, c_:c_ + 1].to_broadcast([128, 32]),
                gtab[:, c_ - 1:c_].to_broadcast([128, 32]), AOT.subtract))
            P.op("vector", lambda e: e.tensor_tensor(fmask[:], fmask[:], dG[:],
                                                     AOT.mult))
            P.op("vector", lambda e: e.tensor_tensor(acc[:], acc[:], fmask[:],
                                                     AOT.add))
        P.op("vector", lambda e: e.tensor_tensor(idx_f[:], iotaj[:], acc[:],
                                                 AOT.add))
        v_idx = P.op("vector", lambda e: e.tensor_copy(idx_i[:], idx_f[:]), sV)
        if debug:
            P.wait("gpsimd", sV, v_idx)
            P.dma("gpsimd", cDBG, dbg["d_idx"], idx_f[:])

        P.settle("gpsimd", cPOS)
        v_own = None
        vprev_tr = None
        for (src_d, dst_i) in [(soffs_d, sidx_own), (pos_d, spos_own)]:
            P.wait("gpsimd", sP, vprev_tr)
            t_g = P.op("gpsimd",
                       lambda e, s=src_d: e.indirect_dma_start(
                           out=own_f[:], out_offset=None, in_=s,
                           in_offset=IndirectOffsetOnAxis(own32[:], 0)), cOWNG, 16)
            P.wait("tensor", cOWNG, t_g)
            P.wait("tensor", sV, v_own)
            P.pe( lambda e: e.transpose(out=pF_32[:], in_=own_f[:],
                                                 identity=ident[0:32, 0:32]))
            vprev_tr = P.op("tensor", lambda e: e.mm_barrier() or e.nop(), sP)
            P.wait("vector", sP, vprev_tr)
            v_own = P.op("vector", lambda e, d=dst_i: e.tensor_copy(d[:], pF_32[:]),
                         sV)
        P.hz = False
        stF.close()
        if phases <= 6:
            raise _Stop()

        phase_fence()
        stG = mkstack()
        nsc = [sbt(stG, f"nsc{i}", [128, D], BF16) for i in range(2)]
        P.settle("sync", pFN[0])
        P.settle("sync", pFN[1])
        P.wait("sync", sV, v_own)
        t_sca = [None, None]
        for k in range(32):
            bx = k % 2
            P.waitt("sync", t_sca[bx])
            t_l = P.dma("sync", pNSC[bx], nsc[bx][:],
                        norm_d[k * 128:(k + 1) * 128, :])
            P.waitt("gpsimd", t_l)
            vsc = P.op("gpsimd",
                       lambda e, b=bx, k=k: e.indirect_dma_start(
                           out=ag_nm_in.ap(),
                           out_offset=IndirectOffsetOnAxis(sidx_own[:, k:k + 1], 0),
                           in_=nsc[b][:], in_offset=None,
                           bounds_check=bregS, oob_is_err=False), pSCA[bx], 16)
            t_sca[bx] = (pSCA[bx], vsc)
        P.settle("gpsimd", pSCA[0])
        P.settle("gpsimd", pSCA[1])
        v_ag2 = P.op("gpsimd", lambda e: e.collective_compute(
            "AllGather", AOT.bypass, replica_groups=[list(range(8))],
            ins=[ag_nm_in.ap().opt()], outs=[ag_nm_out.ap().opt()]), cc)
        stG.close()

        phase_fence()
        stH = mkstack()
        w1_sb = sbt(stH, "w1_sb", [128, 8 * FF], BF16)
        w2_sb = sbt(stH, "w2_sb", [128, 32 * D], BF16)
        P.dma("gpsimd", cW12, w1_sb[:], w1e.rearrange("(c p) f -> p c f", p=128))
        P.dma("gpsimd", cW12, w2_sb[:], w2e.rearrange("(c p) d2 -> p c d2", p=128))
        sgb = [sbt(stH, f"sgb{i}", [128, D], BF16) for i in range(2)]
        stagT = [sbt(stH, f"stagT{i}", [128, 8 * 128], BF16) for i in range(2)]
        identb3 = sbt(stH, "identb3", [128, 128], BF16)
        ptxH = [pst(stH, f"ptxH{i}", [128, 128], BF16) for i in range(2)]
        v_idb3 = P.op("vector", lambda e: e.tensor_copy(identb3[:], ident[:]), sV)
        P.wait("tensor", sV, v_idb3)
        P.wait("gpsimd", cc, v_ag2)
        v_sgb_free = [None, None]
        v_ptxH_ev = [None, None]
        t_stg = [None, None]
        t_sdb = [None, None]
        for k in range(32):
            bx = k % 2
            P.wait("gpsimd", sMM, v_sgb_free[bx])
            P.waitt("gpsimd", t_sdb[bx])
            vg = P.op("gpsimd",
                      lambda e, b=bx, k=k: e.indirect_dma_start(
                          out=sgb[b][:], out_offset=None, in_=ag_nm_out.ap(),
                          in_offset=IndirectOffsetOnAxis(idx_i[:, k:k + 1], 0),
                          bounds_check=bregT, oob_is_err=False), pSGB[bx], 16)
            if debug:
                P.waitt("gpsimd", (pSGB[bx], vg))
                t_sdb[bx] = P.dma("gpsimd", cHRT,
                                  slab_d[k * 128:(k + 1) * 128, :], sgb[bx][:])
            P.wait("tensor", pSGB[bx], vg)
            P.waitt("vector", t_stg[bx])
            for dc in range(8):
                pb = dc % 2
                P.wait("tensor", sV, v_ptxH_ev[pb])
                ntr = P.pe(
                     lambda e, b=bx, dc=dc, pb=pb: e.transpose(
                         out=ptxH[pb][:], in_=sgb[b][:, dc * 128:(dc + 1) * 128],
                         identity=identb3[:]))
                P.wait("vector", sMM, ntr)
                v_ptxH_ev[pb] = P.op(
                    "vector",
                    lambda e, b=bx, dc=dc, pb=pb: e.tensor_copy(
                        stagT[b][:, dc * 128:(dc + 1) * 128], ptxH[pb][:]), sV)
            v_sgb_free[bx] = ntr
            P.wait("gpsimd", sV, v_ptxH_ev[1])
            dstv = slabT_d.rearrange("p (c t) -> p c t", c=8)[:, :,
                                                             k * 128:(k + 1) * 128]
            srcv = stagT[bx][:].rearrange("p (c t) -> p c t", c=8)
            t_stg[bx] = P.dma("gpsimd", pSLB[bx], dstv, srcv)

        nT = [sbt(stH, f"nT{i}", [128, 8 * 512], BF16) for i in range(2)]
        gT = sbt(stH, "gT", [128, 32 * 512], BF16)
        h2T = sbt(stH, "h2T", [128, 8 * 512], BF16)
        hrowb = [sbt(stH, f"hrowb{i}", [128, D], BF16) for i in range(2)]
        pH1 = [pst(stH, f"pH1{i}", [128, 512]) for i in range(2)]
        pH2 = pst(stH, "pH2", [128, 1024])

        P.settle("sync", pSLB[0])
        P.settle("sync", pSLB[1])
        for dc in range(8):
            P.dma("sync", pNT[0], nT[0][:, dc * 512:(dc + 1) * 512],
                  slabT_d[:, dc * CAP: dc * CAP + 512])
        P.settle("tensor", cW12)
        v_gelu = [None, None]
        v_h2ev = None
        t_hrst = [None, None]
        v_hrow_bar = None
        pend_hrow = []
        v_mm1_end = [None, None]
        for s_ in range(8):
            if s_ < 7:
                P.wait("sync", sP, v_mm1_end[(s_ + 1) % 2])
                for dc in range(8):
                    P.dma("sync", pNT[(s_ + 1) % 2],
                          nT[(s_ + 1) % 2][:, dc * 512:(dc + 1) * 512],
                          slabT_d[:, dc * CAP + (s_ + 1) * 512:
                                  dc * CAP + (s_ + 2) * 512])
            P.wait("tensor", pNT[s_ % 2], 16 * 8 * (s_ // 2 + 1))
            nTb = nT[s_ % 2]
            for fc in range(32):
                if fc == 8:
                    while pend_hrow:
                        pend_hrow.pop(0)()
                pb = fc % 2
                P.wait("tensor", sA, v_gelu[pb])
                for dc in range(8):
                    lhs = w1_sb[:, dc * FF + fc * 128: dc * FF + (fc + 1) * 128]
                    P.pe(
                         lambda e, o=pH1[pb], l=lhs,
                                r=nTb[:, dc * 512:(dc + 1) * 512],
                                s0=(dc == 0), s1=(dc == 7):
                         e.matmul(out=o[:], lhsT=l, rhs=r, start=s0, stop=s1))
                P.wait("scalar", sMM, P.npe)
                v_gelu[pb] = P.op("scalar",
                                  lambda e, fc=fc, pb=pb: e.activation(
                                      out=gT[:, fc * 512:(fc + 1) * 512],
                                      in_=pH1[pb][:], func=AF.Gelu_apprx_tanh,
                                      bias=b1s[:, fc:fc + 1]), sA)
            v_mm1_end[s_ % 2] = P.op("tensor",
                                     lambda e: e.mm_barrier() or e.nop(), sP)
            for dc in range(8):
                hb = dc % 2
                if dc == 0:
                    P.wait("tensor", sA, v_gelu[0])
                    P.wait("tensor", sA, v_gelu[1])
                if v_h2ev is not None:
                    P.wait("tensor", sA, v_h2ev)
                for fc in range(32):
                    lhs = w2_sb[:, fc * D + dc * 128: fc * D + (dc + 1) * 128]
                    P.pe(
                         lambda e, o=pH2[:, hb * 512:(hb + 1) * 512], l=lhs,
                                r=gT[:, fc * 512:(fc + 1) * 512],
                                s0=(fc == 0), s1=(fc == 31):
                         e.matmul(out=o, lhsT=l, rhs=r, start=s0, stop=s1))
                P.wait("scalar", sMM, P.npe)
                if dc == 0:
                    P.wait("scalar", sMM, v_hrow_bar)
                v_h2ev = P.op("scalar",
                              lambda e, dc=dc, hb=hb: e.activation(
                                  out=h2T[:, dc * 512:(dc + 1) * 512],
                                  in_=pH2[:, hb * 512:(hb + 1) * 512],
                                  func=AF.Identity, bias=b2s[:, dc:dc + 1]), sA)
            def emit_hrow(s_=s_, v_h2ev=v_h2ev):
                nonlocal v_hrow_bar
                P.wait("tensor", sA, v_h2ev)
                for ts in range(4):
                    hb = ts % 2
                    P.waitt("vector", t_hrst[hb])
                    for dc in range(8):
                        pb = dc % 2
                        P.wait("tensor", sV, v_ptxH_ev[pb])
                        nhr = P.pe(
                             lambda e, ts=ts, dc=dc, pb=pb: e.transpose(
                                 out=ptxH[pb][:],
                                 in_=h2T[:, dc * 512 + ts * 128:
                                         dc * 512 + (ts + 1) * 128],
                                 identity=identb3[:]))
                        P.wait("vector", sMM, nhr)
                        v_ptxH_ev[pb] = P.op(
                            "vector",
                            lambda e, hb=hb, dc=dc, pb=pb: e.tensor_copy(
                                hrowb[hb][:, dc * 128:(dc + 1) * 128],
                                ptxH[pb][:]), sV)
                    P.wait("gpsimd", sV, v_ptxH_ev[1])
                    t_hrst[hb] = P.dma(
                        "gpsimd", pH2T[hb],
                        ag_h_in.ap()[s_ * 512 + ts * 128:
                                     s_ * 512 + (ts + 1) * 128, :],
                        hrowb[hb][:])
                v_hrow_bar = nhr
            pend_hrow.append(emit_hrow)
        while pend_hrow:
            pend_hrow.pop(0)()
        P.settle("gpsimd", pH2T[0])
        P.settle("gpsimd", pH2T[1])
        v_ag3 = P.op("gpsimd", lambda e: e.collective_compute(
            "AllGather", AOT.bypass, replica_groups=[list(range(8))],
            ins=[ag_h_in.ap().opt()], outs=[ag_h_out.ap().opt()]), cc)
        stH.close()
        if phases <= 8:
            raise _Stop()

        phase_fence()
        stI = mkstack()
        NB = 4
        hband = [sbt(stI, f"hband{i}", [128, D], BF16) for i in range(NB)]
        fband = [sbt(stI, f"fband{i}", [128, D]) for i in range(NB)]
        vband = [sbt(stI, f"vband{i}", [128, D]) for i in range(2)]
        oband = [sbt(stI, f"oband{i}", [128, D]) for i in range(NB)]
        sq2 = [sbt(stI, f"sq2{i}", [128, D]) for i in range(2)]
        hf2 = [sbt(stI, f"hf2{i}", [128, D]) for i in range(2)]
        lsum = [sbt(stI, f"lsum{i}", [128, 1]) for i in range(2)]
        lsq = [sbt(stI, f"lsq{i}", [128, 1]) for i in range(2)]
        lmu = [sbt(stI, f"lmu{i}", [128, 1]) for i in range(2)]
        lvar = [sbt(stI, f"lvar{i}", [128, 1]) for i in range(2)]
        lsd = [sbt(stI, f"lsd{i}", [128, 1]) for i in range(2)]
        lrstd = [sbt(stI, f"lrstd{i}", [128, 1]) for i in range(2)]
        P.wait("gpsimd", cc, v_ag3)
        P.wait("sync", cc, v_ag3)
        t_out = []
        t_hb = [None] * NB
        t_fb = [None] * NB
        sHB4 = [pHB[0], pHB[1], pSGB[0], pSGB[1]]
        sFB4 = [pFB[0], pFB[1], pSLB[0], pSLB[1]]
        sOUT4 = [pOUT[0], pOUT[1], pNT[0], pNT[1]]
        vsd_hist = {}
        P.hz = True
        for k in range(32 + 2):
            if k < 32:
                bx = k % NB
                if k >= NB:
                    P.waitt("gpsimd", t_out[k - NB])
                    P.waitt("sync", t_out[k - NB])
                t_hb[bx] = P.op("gpsimd",
                                lambda e, b=bx, k=k: e.indirect_dma_start(
                                    out=hband[b][:], out_offset=None,
                                    in_=ag_h_out.ap(),
                                    in_offset=IndirectOffsetOnAxis(
                                        spos_own[:, k:k + 1], 0),
                                    bounds_check=bregT, oob_is_err=False),
                                sHB4[bx], 16)
                t_fb[bx] = P.dma("sync", sFB4[bx], fband[bx][:],
                                 feat_d[k * 128:(k + 1) * 128, :])
            if k < 2:
                continue
            kk_ = k - 2

            def seg1(kk_=kk_):
                bx = kk_ % NB
                px = kk_ % 2
                P.wait("vector", sHB4[bx], t_hb[bx])
                P.waitt("vector", t_fb[bx])
                P.op("vector", lambda e, b=bx, p=px: e.tensor_copy(
                    hf2[p][:], hband[b][:]))
                P.op("vector", lambda e, b=bx, p=px: e.scalar_tensor_tensor(
                    out=vband[p][:], in0=fband[b][:], scalar=2.0, in1=hf2[p][:],
                    op0=AOT.mult, op1=AOT.add))
                P.op("vector", lambda e, p=px: e.reduce_sum(lsum[p][:],
                                                            vband[p][:], axis=AXX))
                P.op("vector", lambda e, p=px: e.tensor_tensor(
                    sq2[p][:], vband[p][:], vband[p][:], AOT.mult))
                P.op("vector", lambda e, p=px: e.reduce_sum(lsq[p][:], sq2[p][:],
                                                            axis=AXX))
                P.op("vector", lambda e, p=px: e.tensor_scalar_mul(
                    lmu[p][:], lsum[p][:], 1.0 / D))
                P.op("vector", lambda e, p=px: e.tensor_scalar_mul(
                    lvar[p][:], lsq[p][:], 1.0 / D))
                P.op("vector", lambda e, p=px: e.tensor_tensor(
                    sq2[p][:, 0:1], lmu[p][:], lmu[p][:], AOT.mult))
                vv = P.op("vector", lambda e, p=px: e.tensor_tensor(
                    lvar[p][:], lvar[p][:], sq2[p][:, 0:1], AOT.subtract), sV)
                P.wait("scalar", sV, vv)
                return P.op("scalar", lambda e, p=px: e.activation(
                    out=lsd[p][:], in_=lvar[p][:], func=AF.Sqrt,
                    bias=seps[:]), sA)

            def seg2(kk_, vsd):
                bx = kk_ % NB
                px = kk_ % 2
                P.wait("vector", sA, vsd)
                P.op("vector", lambda e, p=px: e.reciprocal(lrstd[p][:],
                                                            lsd[p][:]))
                vo = P.op("vector", lambda e, b=bx, p=px: e.tensor_scalar(
                    out=oband[b][:], in0=vband[p][:], scalar1=lmu[p][:],
                    scalar2=lrstd[p][:], op0=AOT.subtract, op1=AOT.mult), sV)
                P.wait("sync", sV, vo)
                t_out.append(P.dma("sync", sOUT4[bx],
                                   out_t[kk_ * 128:(kk_ + 1) * 128, :],
                                   oband[bx][:]))

            vsd_k = seg1()
            if kk_ >= 1:
                seg2(kk_ - 1, vsd_hist[kk_ - 1])
            vsd_hist[kk_] = vsd_k
            if kk_ == 31:
                seg2(31, vsd_k)
        P.hz = False
        stI.close()
    except _Stop:
        for _st in reversed(list(_open_stacks)):
            _st.close()

    if debug:
        for s in all_dma_sems:
            P.settle("gpsimd", s)
        if phases >= 3:
            P.dma("gpsimd", cDBG, dbg["d_y"], y_d)
        if phases >= 4:
            P.dma("gpsimd", cDBG, dbg["d_feat"], feat_d)
            P.dma("gpsimd", cDBG, dbg["d_norm"], norm_d)
        if phases >= 5:
            P.dma("gpsimd", cDBG, dbg["d_scores"], ag_sc_in.ap())
        if phases >= 6:
            P.dma("gpsimd", cDBG, dbg["d_agsc"], ag_sc_out.ap())
        if phases >= 8:
            P.dma("gpsimd", cDBG, dbg["d_slab"], slab_d)
            P.dma("gpsimd", cDBG, dbg["d_h"], ag_h_in.ap())

    for s in all_dma_sems:
        P.settle("gpsimd", s)
    P.wait("gpsimd", cc, cc.n)
    P.check_deadlock()

    with nc.Block() as block:
        @block.sync
        def _(e):
            P.replay("sync", e)

        @block.scalar
        def _(e):
            P.replay("scalar", e)

        @block.tensor
        def _(e):
            _g = PEGuard(e, sMM.h)
            P.replay("tensor", _g)
            assert _g._n == P.npe, (_g._n, P.npe)

        @block.vector
        def _(e):
            P.replay("vector", e)

        @block.gpsimd
        def _(e):
            P.replay("gpsimd", e)

    stk.close()
    return nc


def host_inputs(inputs):
    x = np.ascontiguousarray(inputs["L0_attn_x"])
    wq = inputs["L0_attn_w_q"]; wk = inputs["L0_attn_w_k"]
    wv = inputs["L0_attn_w_v"]; wo = inputs["L0_attn_w_o"]
    cent = inputs["L0_ffn_centroids"]
    g = inputs["L0_ffn_ln_g"]; bb = inputs["L0_ffn_ln_b"]
    w1 = inputs["L0_ffn_w1"]; b1 = inputs["L0_ffn_b1"]
    w2 = inputs["L0_ffn_w2"]; b2 = inputs["L0_ffn_b2"]

    inv_freq = (10000.0 ** (-(np.arange(HALF, dtype=np.float64) * 2.0 / DH)))
    t = np.arange(S, dtype=np.float64)
    ang = t[:, None] * inv_freq[None, :]
    cos_t = np.cos(ang).astype(np.float32)
    sin_t = np.sin(ang).astype(np.float32)

    ident = np.eye(128, dtype=np.float32)
    utfull = (np.arange(128)[:, None] < np.arange(128)[None, :]).astype(np.float32)
    utown = (utfull * (np.arange(128)[:, None] // 16 ==
                       np.arange(128)[None, :] // 16)).astype(np.float32)
    oind = (np.arange(128)[:, None] // 16 == np.arange(8)[None, :]).astype(np.float32)
    o8t = np.ascontiguousarray(oind.T)
    u8 = (np.arange(8)[:, None] < np.arange(8)[None, :]).astype(np.float32)
    onesf = np.ones((128, 128), np.float32)
    m0 = np.zeros((2, 128), np.float32); m0[0, :] = 1.0
    m1 = np.zeros((2, 128), np.float32); m1[1, :] = 1.0
    iotaj = (np.arange(128)[:, None] + 128 * np.arange(32)[None, :]).astype(np.float32)
    masks = np.zeros((128, 4 * 512), np.float32)
    for k in range(4):
        masks[:, k * 512:(k + 1) * 512] = (
            (128 * k + np.arange(128))[:, None] <= np.arange(512)[None, :])
    e4096 = np.tile((np.arange(8) * 4096).astype(np.float32)[None, :], (128, 1))

    common = dict(
        wqT=np.ascontiguousarray(wq.T), wkT=np.ascontiguousarray(wk.T),
        wvT=np.ascontiguousarray(wv.T), woT=np.ascontiguousarray(wo.T),
        centT=np.ascontiguousarray(cent.T),
        g_rep=np.ascontiguousarray(np.broadcast_to(np.asarray(g, np.float32)[None, :],
                                                   (128, D))),
        b_rep=np.ascontiguousarray(np.broadcast_to(np.asarray(bb, np.float32)[None, :],
                                                   (128, D))),
        cos_in=cos_t, sin_in=sin_t, ident=ident, utfull=utfull, utown=utown,
        oind=oind, o8t=o8t, u8=u8, onesf=onesf, m0=m0, m1=m1, iotaj=iotaj,
        masks=masks, e4096=e4096,
    )
    in_maps = []
    for c in range(8):
        im = dict(common)
        im["x"] = np.ascontiguousarray(x[c])
        im["w1e"] = np.ascontiguousarray(w1[c])
        im["w2e"] = np.ascontiguousarray(w2[c])
        im["b1t"] = np.ascontiguousarray(np.asarray(b1[c]).reshape(FF // 128, 128).T)
        im["b2t"] = np.ascontiguousarray(np.asarray(b2[c]).reshape(D // 128, 128).T)
        im["own32"] = (32 * c + np.arange(32, dtype=np.int32))[:, None].copy()
        im["rowsn"] = np.full((8, 1), c, np.int32)
        im["rowsb2"] = np.full((8, 1), 8 + c, np.int32)
        im["rowsnb"] = np.full((8, 1), 16 + c, np.int32)
        in_maps.append(im)
    return in_maps


def kernel(**inputs):
    in_maps = host_inputs(inputs)
    nc = build_program(debug=_DEBUG)
    res = run_bass_kernel_spmd(nc, in_maps, core_ids=list(range(8)))
    if _DEBUG:
        kernel._last = res
    out = np.stack([np.asarray(r["out"]) for r in res.results], axis=0)
    return out.astype(np.float32)
